# version 04
# baseline (speedup 1.0000x reference)
from contextlib import ExitStack

import numpy as np
import ml_dtypes
import concourse.bass as bass
import concourse.mybir as mybir
from concourse.bass_utils import run_bass_kernel_spmd

F32 = mybir.dt.float32
BF16 = mybir.dt.bfloat16
AF = mybir.ActivationFunctionType
ALU = mybir.AluOpType
AX = mybir.AxisListType

NCORES = 8
B_CORE = 2
S = 2048
D = 1024
NT = S // 128
H, KVH, HD = 16, 4, 64
EPS_LN = 1e-5
EPS_RMS = 1e-6
BASE_G, BASE_L = 1e6, 1e4
TB = 1024
NFT = (B_CORE * S) // TB

HL0 = [0, 1, 2, 3, 8, 9, 10, 11]
HL1 = [4, 5, 6, 7, 12, 13, 14, 15]
QCHUNK = {}
QOFF = {}
for _c in range(8):
    QCHUNK[HL0[_c]] = _c
    QOFF[HL0[_c]] = 0
    QCHUNK[HL1[_c]] = _c
    QOFF[HL1[_c]] = 64
HPERM = []
for _c in range(8):
    HPERM += list(range(HL0[_c] * HD, HL0[_c] * HD + HD))
    HPERM += list(range(HL1[_c] * HD, HL1[_c] * HD + HD))

ENGS = ("tensor", "vector", "scalar", "sync", "gpsimd")


class Sched:
    def __init__(self, nc, ctx):
        self.nc = nc
        self.q = {e: [] for e in ENGS}
        self.sem = {}
        self.cnt = {}
        self.ctx = ctx

    def new_sem(self, name):
        self.sem[name] = self.ctx.enter_context(self.nc.semaphore(name))
        self.cnt[name] = 0

    def emit(self, eng, method, *args, wait=(), inc=None, incn=1, **kw):
        waits = [(self.sem[s], v) for (s, v) in wait if v > 0]
        ih = None
        if inc is not None:
            self.cnt[inc] += incn
            ih = (self.sem[inc], incn)

        def fn(e, method=method, args=args, kw=kw, waits=waits, ih=ih):
            for s, v in waits:
                e.wait_ge(s, v)
            ins = getattr(e, method)(*args, **kw)
            if ih is not None:
                ins.then_inc(ih[0], ih[1])

        self.q[eng].append(fn)
        return self.cnt[inc] if inc is not None else None

    def waitonly(self, eng, wait):
        waits = [(self.sem[s], v) for (s, v) in wait if v > 0]

        def fn(e, waits=waits):
            for s, v in waits:
                e.wait_ge(s, v)

        self.q[eng].append(fn)

    def c(self, name):
        return self.cnt[name]


def build_program(upto="full", dumps=()):
    nc = bass.Bass("TRN2", target_bir_lowering=False, debug=False)
    dt = nc.dram_tensor
    dump_aps = {}
    for name, shape, dty in dumps:
        dump_aps[name] = dt("DBG_" + name, shape, dty, kind="ExternalOutput").ap()

    X = dt("X", [B_CORE * S, D], F32, kind="ExternalInput").ap()
    GWQ = dt("GWQ", [D, D], F32, kind="ExternalInput").ap()
    GKVD = dt("GKVD", [D, 256], F32, kind="ExternalInput").ap()
    GKUP = dt("GKUP", [256, 256], F32, kind="ExternalInput").ap()
    GVUP = dt("GVUP", [256, 256], F32, kind="ExternalInput").ap()
    GWO = dt("GWO", [D, D], F32, kind="ExternalInput").ap()
    LWQ = dt("LWQ", [D, D], F32, kind="ExternalInput").ap()
    LWK = dt("LWK", [D, 256], F32, kind="ExternalInput").ap()
    LWV = dt("LWV", [D, 256], F32, kind="ExternalInput").ap()
    LWO = dt("LWO", [D, D], F32, kind="ExternalInput").ap()
    W1B = dt("W1B", [32, 8, 128, 128], F32, kind="ExternalInput").ap()
    W2B = dt("W2B", [32, 8, 128, 128], F32, kind="ExternalInput").ap()
    H1B = dt("H1B", [32, 32, 128, 128], F32, kind="ExternalInput").ap()
    H2B = dt("H2B", [32, 32, 128, 128], F32, kind="ExternalInput").ap()
    W3B = dt("W3B", [32, 128, D], F32, kind="ExternalInput").ap()
    S1B = dt("S1B", [128, 32], F32, kind="ExternalInput").ap()
    S2B = dt("S2B", [128, 32], F32, kind="ExternalInput").ap()
    H1SB = dt("H1SB", [128, 32], F32, kind="ExternalInput").ap()
    H2SB = dt("H2SB", [128, 32], F32, kind="ExternalInput").ap()
    S3R = dt("S3R", [128, D], F32, kind="ExternalInput").ap()
    N1B = dt("N1B", [128, 8], F32, kind="ExternalInput").ap()
    N2B = dt("N2B", [128, 8], F32, kind="ExternalInput").ap()
    RGC = dt("RGC", [128, NT * HD], F32, kind="ExternalInput").ap()
    RGS = dt("RGS", [128, NT * HD], F32, kind="ExternalInput").ap()
    RLC = dt("RLC", [128, NT * HD], F32, kind="ExternalInput").ap()
    RLS = dt("RLS", [128, NT * HD], F32, kind="ExternalInput").ap()
    MGE = dt("MGE", [128, 128], BF16, kind="ExternalInput").ap()
    MLT = dt("MLT", [128, 128], BF16, kind="ExternalInput").ap()

    OUT = dt("OUT", [B_CORE * S, D], F32, kind="ExternalOutput").ap()

    TSCR = dt("TSCR", [S, D], BF16, kind="Internal").ap()
    KSCR = dt("KSCR", [S, 256], BF16, kind="Internal").ap()
    X2SCR = dt("X2SCR", [B_CORE * S, D], F32, kind="Internal").ap()
    CW1 = dt("CW1", [32, 128, 8, 128], BF16, kind="Internal").ap()
    CW2 = dt("CW2", [32, 128, 8, 128], BF16, kind="Internal").ap()
    CH1 = dt("CH1", [32, 128, 32, 128], BF16, kind="Internal").ap()
    CH2 = dt("CH2", [32, 128, 32, 128], BF16, kind="Internal").ap()
    CW3 = dt("CW3", [32, 128, D], BF16, kind="Internal").ap()

    ctx = ExitStack()
    sb = lambda n, s, d: ctx.enter_context(nc.sbuf_tensor(n, s, d))
    psum = lambda n, s, d: ctx.enter_context(nc.psum_tensor(n, s, d))

    with ctx:
        big1 = sb("big1", [128, 16384], F32)
        big2 = sb("big2", [128, 32768], BF16)
        big3 = sb("big3", [128, 14400], BF16)
        big4 = sb("big4", [128, 6144], F32)
        xst = sb("xst", [128, 2 * D], F32)
        wfst = sb("wfst", [128, 2 * D], F32)
        stgb = sb("stgb", [128, 2 * D + 64], BF16)
        outst = sb("outst", [128, D], F32)
        mge = sb("mge", [128, 128], BF16)
        mlt = sb("mlt", [128, 128], BF16)
        n1b = sb("n1b", [128, 8], F32)
        n2b = sb("n2b", [128, 8], F32)
        ss16 = sb("ss16", [128, 32], F32)
        rq16 = sb("rq16", [128, 16], F32)
        stat = sb("stat", [128, 16], F32)

        hbuf = big1
        x1t = big2[:, 0:16384]
        wbig = big2[:, 16384:24576]
        ktb = big2[:, 24576:28672]
        latb = big2[:, 28672:32768]
        vaug = big3[:, 0:4160]
        aoutT = big3[:, 4160:8256]
        probsT = big3[:, 8256:10304]
        wkvb = big3[:, 10304:12352]
        kupb = big3[:, 12352:12864]
        vupb = big3[:, 12864:13376]
        q2b = big3[:, 13376:14400]
        rgc = big4[:, 0:1024]
        rgs = big4[:, 1024:2048]
        rlc = big4[:, 2048:3072]
        rls = big4[:, 3072:4096]
        swb = big4[:, 4096:5120]
        m1b = big4[:, 5120:6144]
        wotmp = swb
        lnz = m1b[0:1, 0:512]
        rzbs = m1b[0:64, 512:1024]
        rzb16 = swb[0:1, 0:256].bitcast(BF16)
        ones64 = stgb[0:1, 2048:2112]

        hab = big1[:].bitcast(BF16)
        g1b = big2
        hwsl = big3[:, 0:8192]
        w3sl = big3[:, 8192:12288]
        bbuf = big3[:, 12288:13312]
        w1sl = wfst[:, 0:1024].bitcast(BF16)
        w2sl = wfst[:, 1024:2048].bitcast(BF16)
        ysb = big4[:, 0:1024]
        ytmp = big4[:, 1024:2048]
        s3r = big4[:, 2048:3072]
        s1b = big4[:, 3072:3104]
        s2b = big4[:, 3104:3136]
        h1sb = big4[:, 3136:3168]
        h2sb = big4[:, 3168:3200]

        p01 = psum("p01", [128, 1024], F32)
        p23 = psum("p23", [128, 1024], F32)
        p45 = psum("p45", [128, 1024], F32)
        p67 = psum("p67", [128, 1024], F32)

        P = Sched(nc, ctx)
        for s_ in ("ini", "xl0", "xl1", "stg", "trl", "aw0", "aw1", "cv",
                   "mm", "sx", "vx", "out", "x2w", "dbg",
                   "wA0", "wA1", "wB0", "wB1", "wC0", "wC1",
                   "wD0", "wD1", "wD2", "wD3"):
            P.new_sem(s_)
        L = {}

        def dump(name, src_ap, wait=()):
            if name in dump_aps:
                P.emit("sync", "dma_start", inc="dbg", incn=16,
                       out=dump_aps[name][:], in_=src_ap, wait=list(wait))

        def finish_truncated():
            P.waitonly("sync", [("dbg", P.c("dbg")), ("mm", P.c("mm")),
                                ("vx", P.c("vx")), ("sx", P.c("sx")),
                                ("stg", P.c("stg")), ("trl", P.c("trl")),
                                ("cv", P.c("cv"))])

        for dst, src in [(rgc, RGC), (rgs, RGS), (rlc, RLC), (rls, RLS),
                         (mge, MGE), (mlt, MLT), (n1b, N1B), (n2b, N2B)]:
            P.emit("sync", "dma_start", inc="ini", incn=16, out=dst[:], in_=src[:])
        P.waitonly("sync", [("ini", P.c("ini"))])
        P.emit("vector", "memset", ones64, 1.0, inc="vx")
        P.emit("vector", "memset", stat[:, 0:1], EPS_LN, inc="vx")
        P.emit("vector", "memset", stat[:, 1:2], EPS_RMS, inc="vx")
        L["ones_v"] = P.c("vx")

        conv_plan = []
        for tag, src, dst_, n_outer in (("w1", W1B, CW1, 32),
                                        ("w2", W2B, CW2, 32),
                                        ("h1", H1B, CH1, 32),
                                        ("h2", H2B, CH2, 32),
                                        ("w3", W3B, CW3, 32)):
            for o in range(n_outer):
                conv_plan.append((tag, src, dst_, o))
        conv_idx = {}
        nconv = 0
        for tag, src, dst_, o in conv_plan:
            nconv += 1
            conv_idx[(tag, o)] = 16 * nconv
            if tag == "w3":
                ins_, outs_ = src[o], dst_[o]
            else:
                ins_, outs_ = src[o].transpose([1, 0, 2]), dst_[o]
            P.emit("gpsimd", "dma_start", inc="cv", incn=16, out=outs_, in_=ins_)
            P.waitonly("gpsimd", [("cv", 16 * nconv)])

        def stats_ln(src_ap, eps, wait_s=(), wait_v=()):
            P.emit("scalar", "activation", q2b[:], src_ap, AF.Square,
                   wait=list(wait_s) + [("vx", P.c("vx"))],
                   accum_out=stat[:, 8:9], inc="sx")
            P.emit("vector", "reduce_sum", stat[:, 4:5], src_ap, AX.X,
                   wait=list(wait_v), inc="vx")
            P.emit("scalar", "mul", stat[:, 14:15], stat[:, 4:5], 1.0 / D,
                   wait=[("vx", P.c("vx"))], inc="sx")
            P.emit("scalar", "activation", stat[:, 10:11], stat[:, 14:15],
                   AF.Square, inc="sx")
            P.emit("scalar", "mul", stat[:, 10:11], stat[:, 10:11], -1.0,
                   inc="sx")
            P.emit("scalar", "mul", stat[:, 11:12], stat[:, 8:9], 1.0 / D,
                   inc="sx")
            P.emit("scalar", "add", stat[:, 11:12], stat[:, 11:12],
                   stat[:, 10:11], inc="sx")
            P.emit("scalar", "activation", stat[:, 12:13], stat[:, 11:12], AF.Ln,
                   bias=stat[:, 0:1], inc="sx")
            return P.emit("scalar", "activation", stat[:, 15:16],
                          stat[:, 12:13], AF.Exp, scale=-0.5, inc="sx")

        def stats_rms(src_ap, eps, wait_s=()):
            P.emit("scalar", "activation", q2b[:], src_ap, AF.Square,
                   wait=list(wait_s) + [("vx", P.c("vx"))],
                   accum_out=stat[:, 8:9], inc="sx")
            P.emit("scalar", "activation", stat[:, 12:13], stat[:, 8:9], AF.Ln,
                   scale=1.0 / D, bias=stat[:, 1:2], inc="sx")
            return P.emit("scalar", "activation", stat[:, 13:14],
                          stat[:, 12:13], AF.Exp, scale=-0.5, inc="sx")

        def conv_weight(dst_ap, src_ap, scale_ap, sl):
            aw = f"aw{sl}"
            wcols = src_ap.shape[1]
            P.emit("sync", "dma_start", inc=aw, incn=16,
                   out=wfst[:, sl * D:sl * D + wcols], in_=src_ap,
                   wait=[("vx", L.get(f"awv{sl}", 0)), ("mm", L.get("wconv_mm", 0))])
            if scale_ap is not None:
                P.emit("vector", "tensor_scalar", dst_ap,
                       wfst[:, sl * D:sl * D + wcols], scale_ap, None, ALU.mult,
                       wait=[(aw, P.c(aw))], inc="vx")
            else:
                P.emit("vector", "tensor_copy", dst_ap,
                       wfst[:, sl * D:sl * D + wcols],
                       wait=[(aw, P.c(aw))], inc="vx")
            L[f"awv{sl}"] = P.c("vx")

        def rope_qknorm(psrc, width, nheads, tc, ts_, t, stg_off, scr, mm_w,
                        trl_guard):
            P.emit("scalar", "copy", swb[:, 0:width:2], psrc[:, 1:width:2],
                   wait=[("mm", mm_w), ("vx", P.c("vx"))], inc="sx")
            P.emit("scalar", "copy", swb[:, 1:width:2], psrc[:, 0:width:2],
                   inc="sx")
            swdone = P.c("sx")
            cobc = tc[:, t * HD:(t + 1) * HD].unsqueeze(1).broadcast_to(
                [128, nheads, HD])
            sibc = ts_[:, t * HD:(t + 1) * HD].unsqueeze(1).broadcast_to(
                [128, nheads, HD])
            r3 = lambda ap: ap.rearrange("p (h d) -> p h d", d=HD)
            P.emit("vector", "tensor_tensor", r3(m1b[:, 0:width]), r3(psrc),
                   cobc, ALU.mult, wait=[("mm", mm_w), ("sx", swdone)], inc="vx")
            P.emit("vector", "tensor_tensor", r3(swb[:, 0:width]),
                   r3(swb[:, 0:width]), sibc, ALU.mult, inc="vx")
            P.emit("vector", "tensor_tensor", m1b[:, 0:width], m1b[:, 0:width],
                   swb[:, 0:width], ALU.add, inc="vx")
            qr_v = P.c("vx")
            P.emit("scalar", "activation", q2b[:, 0:width], m1b[:, 0:width],
                   AF.Square, wait=[("vx", qr_v)], inc="sx")
            P.emit("vector", "tensor_reduce", ss16[:, 0:nheads],
                   r3(q2b[:, 0:width]), AX.X, ALU.add,
                   wait=[("sx", P.c("sx"))], inc="vx")
            P.emit("scalar", "activation", ss16[:, 16:16 + nheads],
                   ss16[:, 0:nheads], AF.Ln, scale=1.0 / HD, bias=stat[:, 1:2],
                   wait=[("vx", P.c("vx"))], inc="sx")
            P.emit("scalar", "activation", rq16[:, 0:nheads],
                   ss16[:, 16:16 + nheads], AF.Exp, scale=-0.5, inc="sx")
            P.emit("vector", "tensor_tensor",
                   r3(stgb[:, stg_off:stg_off + width]), r3(m1b[:, 0:width]),
                   rq16[:, 0:nheads].unsqueeze(2).broadcast_to([128, nheads, HD]),
                   ALU.mult,
                   wait=[("sx", P.c("sx")), ("stg", L.get(f"stgf{stg_off}", 0))],
                   inc="vx")
            P.emit("sync", "dma_start", inc="stg", incn=16,
                   out=scr[t * 128:(t + 1) * 128, 0:width],
                   in_=stgb[:, stg_off:stg_off + width],
                   wait=[("vx", P.c("vx")), ("trl", trl_guard)])
            L[f"stgf{stg_off}"] = P.c("stg")
            return P.c("vx"), P.c("sx")

        psq = (p01, p23)
        pkv = (p45, p67)

        def attn_weights_global():
            L["wconv_mm"] = P.c("mm")
            for c in range(8):
                conv_weight(wbig[:, c * D:(c + 1) * D],
                            GWQ[c * 128:(c + 1) * 128, :], n1b[:, c:c + 1], c % 2)
            for c in range(8):
                conv_weight(wkvb[:, c * 256:(c + 1) * 256],
                            GKVD[c * 128:(c + 1) * 128, :], n1b[:, c:c + 1], c % 2)
            for (dst, src) in ((kupb, GKUP), (vupb, GVUP)):
                for c in range(2):
                    conv_weight(dst[:, c * 256:(c + 1) * 256],
                                src[c * 128:(c + 1) * 128, :], None, c % 2)
            L["wbig_ready"] = P.c("vx")

        def attn_weights_local():
            L["wconv_mm"] = P.c("mm")
            for c in range(8):
                conv_weight(wbig[:, c * D:(c + 1) * D],
                            LWQ[c * 128:(c + 1) * 128, :], n2b[:, c:c + 1], c % 2)
            for (ofs, src) in ((0, LWK), (2048, LWV)):
                for c in range(8):
                    conv_weight(latb[:, ofs + c * 256:ofs + (c + 1) * 256],
                                src[c * 128:(c + 1) * 128, :], n2b[:, c:c + 1],
                                c % 2)
            L["wbig_ready"] = P.c("vx")
            L["lkv_ready"] = P.c("vx")

        def load_wo(src):
            L["wconv_mm"] = P.c("mm")
            for c in range(8):
                conv_weight(wbig[:, c * D:(c + 1) * D],
                            src[c * 128:(c + 1) * 128, :], None, c % 2)
            L["wo_ready"] = P.c("vx")

        def ln1_tile(b, t):
            rows0 = b * S
            sl = t % 2
            P.emit("sync", "dma_start", inc=f"xl{sl}", incn=16,
                   out=xst[:, sl * D:(sl + 1) * D],
                   in_=X[rows0 + t * 128:rows0 + (t + 1) * 128, :],
                   wait=[("sx", L.get(f"xst{sl}_s", 0)),
                         ("vx", L.get(f"xst{sl}_v", 0))])
            xlc = P.c(f"xl{sl}")
            sc = stats_ln(xst[:, sl * D:(sl + 1) * D], EPS_LN,
                          wait_s=[(f"xl{sl}", xlc)],
                          wait_v=[(f"xl{sl}", xlc)])
            P.emit("vector", "tensor_scalar", hbuf[:, t * D:(t + 1) * D],
                   xst[:, sl * D:(sl + 1) * D], stat[:, 14:15],
                   stat[:, 15:16], ALU.subtract, ALU.mult,
                   wait=[("sx", sc)], inc="vx")
            L[f"xst{sl}_s"] = P.c("sx")
            L[f"xst{sl}_v"] = P.c("vx")

        def ln1_batch(b):
            for t in range(NT):
                ln1_tile(b, t)
            L["hbuf_v"] = P.c("vx")

        def rms_stage_tile(t, trl_guard):
            htile = hbuf[:, t * D:(t + 1) * D]
            sc = stats_rms(htile, EPS_RMS)
            off = (t % 2) * D
            P.emit("vector", "tensor_scalar", stgb[:, off:off + D],
                   htile, stat[:, 13:14], None, ALU.mult,
                   wait=[("sx", sc), ("stg", L.get(f"stgf{off}", 0))],
                   inc="vx")
            P.emit("sync", "dma_start", inc="stg", incn=16,
                   out=TSCR[t * 128:(t + 1) * 128, :],
                   in_=stgb[:, off:off + D],
                   wait=[("vx", P.c("vx")), ("trl", trl_guard)])
            L[f"stgf{off}"] = P.c("stg")

        def x2_out_tile(b, t):
            rows0 = b * S
            sl = t % 2
            P.emit("sync", "dma_start", inc=f"xl{sl}", incn=16,
                   out=xst[:, sl * D:(sl + 1) * D],
                   in_=X[rows0 + t * 128:rows0 + (t + 1) * 128, :],
                   wait=[("sx", L.get(f"xst{sl}_s", 0)),
                         ("vx", L.get(f"xst{sl}_v", 0))])
            xlc = P.c(f"xl{sl}")
            htile = hbuf[:, t * D:(t + 1) * D]
            sc = stats_ln(htile, EPS_LN)
            P.emit("vector", "tensor_scalar", wotmp[:], htile,
                   stat[:, 14:15], stat[:, 15:16], ALU.subtract, ALU.mult,
                   wait=[("sx", sc)], inc="vx")
            P.emit("vector", "tensor_tensor", outst[:],
                   wotmp[:], xst[:, sl * D:(sl + 1) * D], ALU.add,
                   wait=[(f"xl{sl}", xlc),
                         ("x2w", L.get("outst_f", 0))], inc="vx")
            L[f"xst{sl}_s"] = P.c("sx")
            L[f"xst{sl}_v"] = P.c("vx")
            P.emit("sync", "dma_start", inc="x2w", incn=16,
                   out=X2SCR[rows0 + t * 128:rows0 + (t + 1) * 128, :],
                   in_=outst[:], wait=[("vx", P.c("vx"))])
            L["outst_f"] = P.c("x2w")

        def attn_batch(b, glob, prestaged=False, qs_hook=None):
            tc, ts_ = (rgc, rgs) if glob else (rlc, rls)
            trl_now = P.c("trl")
            if not prestaged:
                for t in range(NT):
                    rms_stage_tile(t, trl_now)
                stg_norm = P.c("stg")
            else:
                stg_norm = L.get("prestage_stg", 0)
            if glob:
                dump(f"tscr_{b}", TSCR[:], wait=[("stg", stg_norm)])
            for c in range(8):
                P.emit("sync", "dma_start", inc="trl", incn=16,
                       out=x1t[:, c * S:(c + 1) * S],
                       in_=TSCR[:, c * 128:(c + 1) * 128], transpose=True,
                       wait=[("stg", stg_norm), ("mm", L.get("x1t_free", 0)),
                             ("sx", L.get("x1t_free_s", 0))])
            trl_x1t = P.c("trl")
            if glob:
                dump(f"x1t_{b}", x1t[:], wait=[("trl", trl_x1t)])
                dump(f"wbig_{b}", wbig[:], wait=[("vx", P.c("vx"))])
                dump(f"wkvb_{b}", wkvb[:], wait=[("vx", P.c("vx"))])
            if upto == "x1t_stop" and glob and b == 0:
                return "STOP"  

            if glob:
                for lc in range(2):
                    for tg in range(4):
                        k4 = lc * 4 + tg
                        pp = pkv[k4 % 2]
                        half = (k4 // 2) % 2
                        pslice = pp[:, half * 512:half * 512 + 512]
                        for dc in range(8):
                            w = ()
                            if dc == 0:
                                w = [("trl", trl_x1t),
                                     ("vx", L.get("wbig_ready", 0)),
                                     ("sx", L.get(f"lat{k4 % 4}_free", 0))]
                            P.emit("tensor", "matmul", pslice,
                                   wkvb[:, dc * 256 + lc * 128:
                                        dc * 256 + lc * 128 + 128],
                                   x1t[:, dc * S + tg * 512:dc * S + tg * 512 + 512],
                                   start=(dc == 0), stop=(dc == 7), wait=w,
                                   inc=("mm" if dc == 7 else None))
                        mmc = P.c("mm")
                        P.emit("scalar", "copy",
                               latb[:, lc * S + tg * 512:lc * S + tg * 512 + 512],
                               pslice, wait=[("mm", mmc)], inc="sx")
                        L[f"lat{k4 % 4}_free"] = P.c("sx")
                latdone_s = P.c("sx")
            else:
                latdone_s = P.c("sx")

            for t in range(NT):
                pq_ = psq[t % 2]
                for dc in range(8):
                    w = ()
                    if dc == 0:
                        w = [("trl", trl_x1t), ("vx", L.get("wbig_ready", 0)),
                             ("vx", L.get(f"pq{t % 2}_free_v", 0)),
                             ("sx", L.get(f"pq{t % 2}_free_s", 0))]
                    P.emit("tensor", "matmul", pq_[:, 0:512],
                           x1t[:, dc * S + t * 128:dc * S + t * 128 + 128],
                           wbig[:, dc * D:dc * D + 512],
                           start=(dc == 0), stop=(dc == 7), wait=w)
                    P.emit("tensor", "matmul", pq_[:, 512:1024],
                           x1t[:, dc * S + t * 128:dc * S + t * 128 + 128],
                           wbig[:, dc * D + 512:dc * D + 1024],
                           start=(dc == 0), stop=(dc == 7),
                           inc=("mm" if dc == 7 else None))
                q_mm = P.c("mm")
                pk_ = pkv[t % 2]
                if glob:
                    for lc in range(2):
                        w = ([("sx", latdone_s),
                              ("sx", L.get(f"pk{t % 2}_free_s", 0)),
                              ("vx", L.get(f"pk{t % 2}_free_v", 0))]
                             if lc == 0 else ())
                        P.emit("tensor", "matmul", pk_[:, 0:256],
                               latb[:, lc * S + t * 128:lc * S + t * 128 + 128],
                               kupb[:, lc * 256:(lc + 1) * 256],
                               start=(lc == 0), stop=(lc == 1), wait=w)
                    for lc in range(2):
                        P.emit("tensor", "matmul", pk_[:, 512:768],
                               latb[:, lc * S + t * 128:lc * S + t * 128 + 128],
                               vupb[:, lc * 256:(lc + 1) * 256],
                               start=(lc == 0), stop=(lc == 1),
                               inc=("mm" if lc == 1 else None))
                else:
                    for dc in range(8):
                        w = ([("vx", L.get("lkv_ready", 0)),
                              ("sx", L.get(f"pk{t % 2}_free_s", 0)),
                              ("vx", L.get(f"pk{t % 2}_free_v", 0))]
                             if dc == 0 else ())
                        P.emit("tensor", "matmul", pk_[:, 0:256],
                               x1t[:, dc * S + t * 128:dc * S + t * 128 + 128],
                               latb[:, dc * 256:(dc + 1) * 256],
                               start=(dc == 0), stop=(dc == 7), wait=w)
                    for dc in range(8):
                        P.emit("tensor", "matmul", pk_[:, 512:768],
                               x1t[:, dc * S + t * 128:dc * S + t * 128 + 128],
                               latb[:, 2048 + dc * 256:2048 + (dc + 1) * 256],
                               start=(dc == 0), stop=(dc == 7),
                               inc=("mm" if dc == 7 else None))
                kv_mm = P.c("mm")
                sl = t % 2
                v_, s_ = rope_qknorm(pq_[:, 0:D], D, H, tc, ts_, t, sl * D,
                                     TSCR, q_mm, trl_x1t)
                L[f"pq{t % 2}_free_v"] = v_
                L[f"pq{t % 2}_free_s"] = s_
                v2_, s2_ = rope_qknorm(pk_[:, 0:256], 256, KVH, tc, ts_, t,
                                       sl * D, KSCR, kv_mm, L.get("ktrl", 0))
                P.emit("scalar", "copy",
                       vaug[:, t * 260:t * 260 + 260].rearrange(
                           "p (h c) -> p h c", c=65)[:, :, 0:64],
                       pk_[:, 512:768].rearrange("p (h c) -> p h c", c=64),
                       wait=[("mm", kv_mm), ("mm", L.get("vaug_free", 0))],
                       inc="sx")
                P.emit("vector", "memset",
                       vaug[:, t * 260:t * 260 + 260].rearrange(
                           "p (h c) -> p h c", c=65)[:, :, 64:65], 1.0,
                       wait=[("sx", P.c("sx"))], inc="vx")
                L[f"pk{t % 2}_free_s"] = P.c("sx")
                L[f"pk{t % 2}_free_v"] = max(v2_, P.c("vx"))
            stg_q = P.c("stg")
            proj_mm = P.c("mm")
            L["x1t_free"] = proj_mm

            load_wo(GWO if glob else LWO)

            for c in range(8):
                P.emit("sync", "dma_start", inc="trl", incn=16,
                       out=x1t[:, c * S:(c + 1) * S],
                       in_=TSCR[:, c * 128:(c + 1) * 128], transpose=True,
                       wait=[("stg", stg_q), ("mm", proj_mm)])
            for c in range(2):
                P.emit("sync", "dma_start", inc="trl", incn=16,
                       out=ktb[:, c * S:(c + 1) * S],
                       in_=KSCR[:, c * 128:(c + 1) * 128], transpose=True,
                       wait=[("stg", stg_q), ("mm", L.get("ktb_free", 0))])
            trl_qk = P.c("trl")
            L["ktrl"] = trl_qk

            L["sc0_free_s"] = L["sc1_free_s"] = P.c("sx")
            L["sc0_free_v"] = L["sc1_free_v"] = P.c("vx")
            L["pv_free_v"] = P.c("vx")
            L["pv_free_s"] = P.c("sx")
            pvb = p01
            for qs in range(4):
                for h in range(H):
                    kvh = h // 4
                    qch, qoff = QCHUNK[h], QOFF[h]
                    kch, koff = kvh // 2, 64 * (kvh % 2)
                    if glob:
                        ktcs = list(range(0, 4 * qs + 4))
                    else:
                        ktcs = [k for k in range(4 * qs - 2, 4 * qs + 4) if k >= 0]
                    pairs = [ktcs[i:i + 2] for i in range(0, len(ktcs), 2)]

                    def rng(ktc):
                        o = 128 * ktc - 512 * qs
                        lo = max(0, o)
                        hi = 512 if glob else min(512, o + 384)
                        return o, lo, hi

                    def emit_qk(pi):
                        pj = pairs[pi]
                        pp = pkv[pi % 2]
                        for z, ktc in enumerate(pj):
                            o, lo, hi = rng(ktc)
                            w = ()
                            if z == 0:
                                w = [("trl", trl_qk),
                                     ("sx", L.get(f"sc{pi % 2}_free_s", 0)),
                                     ("vx", L.get(f"sc{pi % 2}_free_v", 0))]
                            P.emit("tensor", "matmul",
                                   pp[:, z * 512 + lo:z * 512 + hi],
                                   ktb[koff:koff + 64,
                                       kch * S + ktc * 128:kch * S + ktc * 128 + 128],
                                   x1t[qoff:qoff + 64,
                                       qch * S + qs * 512 + lo:
                                       qch * S + qs * 512 + hi],
                                   start=True, stop=True, wait=w,
                                   inc=("mm" if z == len(pj) - 1 else None))
                        return P.c("mm")

                    def emit_expv(pi, qk_mm):
                        pj = pairs[pi]
                        pp = pkv[pi % 2]
                        expz = []
                        for z, ktc in enumerate(pj):
                            o, lo, hi = rng(ktc)
                            pb = (pi % 2) * 1024 + z * 512
                            P.emit("scalar", "activation",
                                   probsT[:, pb + lo:pb + hi],
                                   pp[:, z * 512 + lo:z * 512 + hi], AF.Exp,
                                   scale=0.125,
                                   wait=([("mm", qk_mm)] if z == 0 else ()),
                                   inc="sx")
                            expz.append(P.c("sx"))
                        L[f"sc{pi % 2}_free_s"] = P.c("sx")
                        for z, ktc in enumerate(pj):
                            o, lo, hi = rng(ktc)
                            pb = (pi % 2) * 1024 + z * 512
                            wv = 0
                            if o >= 0:
                                P.emit("vector", "tensor_tensor",
                                       probsT[:, pb + o:pb + o + 128],
                                       probsT[:, pb + o:pb + o + 128],
                                       mge[:], ALU.mult,
                                       wait=[("sx", expz[z])], inc="vx")
                                wv = P.c("vx")
                            if (not glob) and 0 <= o + 256 < 512:
                                P.emit("vector", "tensor_tensor",
                                       probsT[:, pb + o + 256:pb + o + 384],
                                       probsT[:, pb + o + 256:pb + o + 384],
                                       mlt[:], ALU.mult,
                                       wait=[("sx", expz[z])], inc="vx")
                                wv = P.c("vx")
                            j = ktcs.index(ktc)
                            w = [("sx", expz[z])]
                            if wv:
                                w.append(("vx", wv))
                            if j == 0:
                                w.append(("vx", L.get("pv_free_v", 0)))
                                w.append(("sx", L.get("pv_free_s", 0)))
                            P.emit("tensor", "matmul", pvb[0:65, lo:hi],
                                   vaug[:, ktc * 260 + 65 * kvh:
                                        ktc * 260 + 65 * kvh + 65],
                                   probsT[:, pb + lo:pb + hi],
                                   start=(j == 0), stop=(j == len(ktcs) - 1),
                                   wait=w,
                                   inc=("mm" if j == len(ktcs) - 1 else None))
                        L[f"sc{pi % 2}_free_v"] = P.c("vx")

                    qkm = [emit_qk(0)]
                    for pi in range(1, len(pairs)):
                        qkm.append(emit_qk(pi))
                        emit_expv(pi - 1, qkm[pi - 1])
                    emit_expv(len(pairs) - 1, qkm[-1])
                    pv_mm = P.c("mm")

                    P.emit("scalar", "activation", lnz, pvb[64:65, 0:512],
                           AF.Ln, wait=[("mm", pv_mm)], inc="sx")
                    P.emit("scalar", "activation", rzb16, lnz, AF.Exp,
                           scale=-1.0, inc="sx")
                    rz_sx = P.c("sx")
                    P.emit("tensor", "matmul", pvb[0:64, 512:1024],
                           ones64, rzb16, start=True, stop=True,
                           wait=[("sx", rz_sx), ("sx", L.get("rzb_free", 0)),
                                 ("vx", L["ones_v"])],
                           inc="mm")
                    P.emit("scalar", "copy", rzbs, pvb[0:64, 512:1024],
                           wait=[("mm", P.c("mm")),
                                 ("vx", L.get("pv_free_v", 0))], inc="sx")
                    L["rzb_free"] = P.c("sx")
                    L["pv_free_s"] = P.c("sx")
                    P.emit("vector", "tensor_tensor",
                           aoutT[qoff:qoff + 64, 512 * qch:512 * qch + 512],
                           pvb[0:64, 0:512], rzbs, ALU.mult,
                           wait=[("sx", P.c("sx")),
                                 ("mm", L.get("aout_free", 0))], inc="vx")
                    L["pv_free_v"] = P.c("vx")
                heads_v = P.c("vx")
                for qb in range(4):
                    pw = (p23, p45)[qb % 2]
                    tglob = 4 * qs + qb
                    for r in range(8):
                        w = ()
                        if r == 0:
                            w = [("vx", heads_v), ("vx", L.get("wo_ready", 0)),
                                 ("sx", P.c("sx"))]
                        P.emit("tensor", "matmul", pw[:, 0:512],
                               aoutT[:, r * 512 + qb * 128:r * 512 + qb * 128 + 128],
                               wbig[:, r * D:r * D + 512],
                               start=(r == 0), stop=(r == 7), wait=w)
                        P.emit("tensor", "matmul", pw[:, 512:1024],
                               aoutT[:, r * 512 + qb * 128:r * 512 + qb * 128 + 128],
                               wbig[:, r * D + 512:r * D + 1024],
                               start=(r == 0), stop=(r == 7),
                               inc=("mm" if r == 7 else None))
                    wo_mm = P.c("mm")
                    P.emit("scalar", "copy", wotmp[:], pw[:],
                           wait=[("mm", wo_mm)], inc="sx")
                    P.emit("vector", "tensor_tensor",
                           hbuf[:, tglob * D:(tglob + 1) * D],
                           hbuf[:, tglob * D:(tglob + 1) * D], wotmp[:],
                           ALU.add, wait=[("sx", P.c("sx"))], inc="vx")
                L["aout_free"] = P.c("mm")
                L["sc0_free_s"] = L["sc1_free_s"] = P.c("sx")
                L["sc0_free_v"] = L["sc1_free_v"] = P.c("vx")
                L["pv_free_v"] = P.c("vx")
                L["pv_free_s"] = P.c("sx")
                if qs_hook is not None:
                    qs_hook(qs)
            L["hbuf_v"] = P.c("vx")
            L["ktb_free"] = P.c("mm")
            L["vaug_free"] = P.c("mm")
            L["x1t_free"] = P.c("mm")
            L["x1t_free_s"] = P.c("sx")
            for i in (0, 1):
                L[f"pq{i}_free_v"] = L[f"pk{i}_free_v"] = P.c("vx")
                L[f"pq{i}_free_s"] = L[f"pk{i}_free_s"] = P.c("sx")


        def ffn_stage_tile(T, trl_guard, x2wait):
            rows0 = T * TB
            for st in range(8):
                sl = st % 2
                P.emit("sync", "dma_start", inc=f"xl{sl}", incn=16,
                       out=xst[:, sl * D:(sl + 1) * D],
                       in_=X2SCR[rows0 + st * 128:rows0 + (st + 1) * 128, :],
                       wait=[("x2w", x2wait), ("sx", L.get(f"xst{sl}_s", 0)),
                             ("vx", L.get(f"xst{sl}_v", 0))])
                xlc = P.c(f"xl{sl}")
                sc = stats_ln(xst[:, sl * D:(sl + 1) * D], EPS_LN,
                              wait_s=[(f"xl{sl}", xlc)],
                              wait_v=[(f"xl{sl}", xlc)])
                off = sl * D
                P.emit("vector", "tensor_scalar", stgb[:, off:off + D],
                       xst[:, sl * D:(sl + 1) * D], stat[:, 14:15],
                       stat[:, 15:16], ALU.subtract, ALU.mult,
                       wait=[("sx", sc), ("stg", L.get(f"stgf{off}", 0))],
                       inc="vx")
                L[f"xst{sl}_s"] = P.c("sx")
                L[f"xst{sl}_v"] = P.c("vx")
                P.emit("sync", "dma_start", inc="stg", incn=16,
                       out=TSCR[st * 128:(st + 1) * 128, :],
                       in_=stgb[:, off:off + D],
                       wait=[("vx", P.c("vx")), ("trl", trl_guard)])
                L[f"stgf{off}"] = P.c("stg")
            L[f"ffnstg{T}"] = P.c("stg")

        def emit_all():
            attn_weights_global()
            for b in range(B_CORE):
                if b == 0:
                    ln1_batch(b)
                dump(f"h_ln1_{b}", hbuf[:], wait=[("vx", P.c("vx"))])
                if upto == f"ln1_{b}":
                    return False

                def hook_g(qs):
                    for t in range(4 * qs, 4 * qs + 4):
                        rms_stage_tile(t, P.c("trl"))
                r = attn_batch(b, glob=True, prestaged=(b > 0), qs_hook=hook_g)
                if r == "STOP":
                    return False
                L["prestage_stg"] = P.c("stg")
                dump(f"h_glob_{b}", hbuf[:], wait=[("vx", P.c("vx"))])
                dump(f"aoutT_{b}", aoutT[:], wait=[("vx", P.c("vx"))])
                dump(f"ktb_{b}", ktb[:], wait=[("trl", P.c("trl"))])
                dump(f"qt_{b}", x1t[:], wait=[("trl", P.c("trl"))])
                dump(f"latb_{b}", latb[:], wait=[("sx", P.c("sx"))])
                dump(f"vaug_{b}", vaug[:], wait=[("sx", P.c("sx")), ("vx", P.c("vx"))])
                if upto == f"glob_{b}":
                    return False
                attn_weights_local()

                def hook_l(qs, b=b):
                    for t in range(4 * qs, 4 * qs + 4):
                        x2_out_tile(b, t)
                        if b + 1 < B_CORE:
                            ln1_tile(b + 1, t)
                            rms_stage_tile(t, P.c("trl"))
                    if b + 1 == B_CORE and qs == 3:
                        ffn_stage_tile(0, P.c("trl"), P.c("x2w"))
                attn_batch(b, glob=False, prestaged=True, qs_hook=hook_l)
                L["prestage_stg"] = P.c("stg")
                dump(f"h_loc_{b}", hbuf[:], wait=[("vx", P.c("vx"))])
                if upto == f"loc_{b}":
                    return False
                if b + 1 < B_CORE:
                    attn_weights_global()
            return True
        cont = emit_all()
        if not cont:
            finish_truncated()
            blockT = ctx.enter_context(nc.Block())
            for _e in ENGS:
                def _mk(_e):
                    def _f(e):
                        for fn in P.q[_e]:
                            fn(e)
                    return _f
                getattr(blockT, _e)(_mk(_e))
            return nc
        x2done = P.c("x2w")
        dump("x2scr", X2SCR[:], wait=[("x2w", x2done)])
        if upto == "x2":
            finish_truncated()
            blockT = ctx.enter_context(nc.Block())
            for _e in ENGS:
                def _mk(_e):
                    def _f(e):
                        for fn in P.q[_e]:
                            fn(e)
                    return _f
                getattr(blockT, _e)(_mk(_e))
            return nc

        P.waitonly("sync", [("mm", P.c("mm")), ("vx", P.c("vx")),
                            ("sx", P.c("sx"))])
        for dst, src in [(s3r, S3R), (s1b, S1B), (s2b, S2B),
                         (h1sb, H1SB), (h2sb, H2SB)]:
            P.emit("sync", "dma_start", inc="ini", incn=16, out=dst[:], in_=src[:])
        ini2 = P.c("ini")
        for k in ("w1sl0_free", "w1sl1_free", "w2sl0_free", "w2sl1_free",
                  "hw0_free", "hw1_free", "w30_free", "w31_free", "x2t_free"):
            L[k] = 0

        pfA = (p01, p23)
        pfB = (p45, p67)

        def x2t_loads(T):
            for c in range(8):
                P.emit("sync", "dma_start", inc="trl", incn=16,
                       out=x1t[:, c * TB:(c + 1) * TB],
                       in_=TSCR[0:TB, c * 128:(c + 1) * 128], transpose=True,
                       wait=[("stg", L[f"ffnstg{T}"]),
                             ("mm", L.get("x2t_free", 0))])
            return P.c("trl")

        for T in range(NFT):
            rows0 = T * TB
            if T == 0:
                if "ffnstg0" not in L:
                    ffn_stage_tile(0, P.c("trl"), P.c("x2w"))
                trl_x2t = x2t_loads(0)
            else:
                trl_x2t = L["x2t_trl"]
            if T + 1 < NFT:
                ffn_stage_tile(T + 1, trl_x2t, P.c("x2w"))

            b1_v = P.c("vx")
            b1_s = P.c("sx")
            for hc in range(32):
                sl = hc % 2
                P.emit("sync", "dma_start", inc=f"wA{sl}", incn=16,
                       out=w1sl[:, sl * 1024:(sl + 1) * 1024],
                       in_=CW1[hc].rearrange("p a b -> p (a b)"),
                       wait=[("cv", conv_idx[("w1", hc)]),
                             ("mm", L.get(f"w1sl{sl}_free", 0))])
                wA = P.c(f"wA{sl}")
                P.emit("sync", "dma_start", inc=f"wB{sl}", incn=16,
                       out=w2sl[:, sl * 1024:(sl + 1) * 1024],
                       in_=CW2[hc].rearrange("p a b -> p (a b)"),
                       wait=[("cv", conv_idx[("w2", hc)]),
                             ("mm", L.get(f"w2sl{sl}_free", 0))])
                wB = P.c(f"wB{sl}")
                pa1, pa2 = (pfA if hc % 2 == 0 else pfB)
                for dc in range(8):
                    for g_ in range(2):
                        w = ()
                        if dc == 0 and g_ == 0:
                            w = [("trl", trl_x2t), (f"wA{sl}", wA),
                                 ("sx", max(L.get(f"pa{hc % 2}_free_s", 0), b1_s)),
                                 ("vx", max(L.get(f"pa{hc % 2}_free_v", 0), b1_v))]
                        P.emit("tensor", "matmul",
                               pa1[:, g_ * 512:(g_ + 1) * 512],
                               w1sl[:, sl * 1024 + dc * 128:
                                    sl * 1024 + dc * 128 + 128],
                               x1t[:, dc * TB + g_ * 512:dc * TB + g_ * 512 + 512],
                               start=(dc == 0), stop=(dc == 7), wait=w,
                               inc=("mm" if (dc == 7 and g_ == 1) else None))
                L[f"w1sl{sl}_free"] = P.c("mm")
                for dc in range(8):
                    for g_ in range(2):
                        w = [(f"wB{sl}", wB)] if (dc == 0 and g_ == 0) else ()
                        P.emit("tensor", "matmul",
                               pa2[:, g_ * 512:(g_ + 1) * 512],
                               w2sl[:, sl * 1024 + dc * 128:
                                    sl * 1024 + dc * 128 + 128],
                               x1t[:, dc * TB + g_ * 512:dc * TB + g_ * 512 + 512],
                               start=(dc == 0), stop=(dc == 7), wait=w,
                               inc=("mm" if (dc == 7 and g_ == 1) else None))
                a_mm = P.c("mm")
                L[f"w2sl{sl}_free"] = a_mm
                P.emit("scalar", "activation", q2b[:], pa1[:], AF.Silu,
                       scale=s1b[:, hc:hc + 1],
                       wait=[("mm", a_mm), ("vx", P.c("vx"))], inc="sx")
                L[f"pa{hc % 2}_free_s"] = P.c("sx")
                P.emit("vector", "tensor_scalar", bbuf[:], pa2[:],
                       s2b[:, hc:hc + 1], None, ALU.mult,
                       wait=[("mm", a_mm)], inc="vx")
                P.emit("vector", "tensor_tensor", hab[:, hc * TB:(hc + 1) * TB],
                       q2b[:], bbuf[:], ALU.mult,
                       wait=[("sx", P.c("sx"))], inc="vx")
                L[f"pa{hc % 2}_free_v"] = P.c("vx")
            ha_v = P.c("vx")

            for stage in range(2):
                tag = "h1" if stage == 0 else "h2"
                srcW = CH1 if stage == 0 else CH2
                scl = h1sb if stage == 0 else h2sb
                src = hab if stage == 0 else g1b
                dst = g1b if stage == 0 else hab
                stage_s = P.c("sx")
                stage_v = P.c("vx")
                for oc in range(32):
                    sl = oc % 2
                    P.emit("sync", "dma_start", inc=f"wC{sl}", incn=16,
                           out=hwsl[:, sl * 4096:(sl + 1) * 4096],
                           in_=srcW[oc].rearrange("p a b -> p (a b)"),
                           wait=[("cv", conv_idx[(tag, oc)]),
                                 ("mm", L.get(f"hw{sl}_free", 0))])
                    wC = P.c(f"wC{sl}")
                    pg = (p01, p23)[oc % 2]
                    for dc in range(32):
                        for g_ in range(2):
                            w = ()
                            if dc == 0 and g_ == 0:
                                w = [(f"wC{sl}", wC), ("vx", stage_v),
                                     ("sx", max(L.get(f"pg{oc % 2}_free", 0),
                                                stage_s))]
                                if stage == 0:
                                    w.append(("vx", ha_v))
                                else:
                                    w.append(("mm", L.get("b2_done", 0)))
                            P.emit("tensor", "matmul",
                                   pg[:, g_ * 512:(g_ + 1) * 512],
                                   hwsl[:, sl * 4096 + dc * 128:
                                        sl * 4096 + dc * 128 + 128],
                                   src[:, dc * TB + g_ * 512:dc * TB + g_ * 512 + 512],
                                   start=(dc == 0), stop=(dc == 31), wait=w,
                                   inc=("mm" if (dc == 31 and g_ == 1) else None))
                    g_mm = P.c("mm")
                    L[f"hw{sl}_free"] = g_mm
                    P.emit("scalar", "activation", dst[:, oc * TB:(oc + 1) * TB],
                           pg[:], AF.Gelu, scale=scl[:, oc:oc + 1],
                           wait=[("mm", g_mm)], inc="sx")
                    L[f"pg{oc % 2}_free"] = P.c("sx")
                if stage == 0:
                    L["b2_done"] = P.c("mm")
            L["x2t_free"] = P.c("mm")
            if T + 1 < NFT:
                L["x2t_trl"] = x2t_loads(T + 1)
            g2_s = P.c("sx")

            b4_s = P.c("sx")
            b4_v = P.c("vx")
            for grp in range(2):
                for dc in range(32):
                    sl = dc % 4
                    P.emit("sync", "dma_start", inc=f"wD{sl}", incn=16,
                           out=w3sl[:, sl * 1024:(sl + 1) * 1024],
                           in_=CW3[dc],
                           wait=[("cv", conv_idx[("w3", dc)]),
                                 ("mm", L.get(f"w3{sl}_free", 0))])
                    wD = P.c(f"wD{sl}")
                    for tb4 in range(4):
                        tb = grp * 4 + tb4
                        pw = (p01, p23, p45, p67)[tb4]
                        for g_ in range(2):
                            w = ()
                            if dc == 0 and g_ == 0:
                                w = [(f"wD{sl}", wD),
                                     ("sx", max(g2_s, b4_s)),
                                     ("vx", max(L.get(f"pw{tb4}_free", 0), b4_v))]
                            elif tb4 == 0 and g_ == 0:
                                w = [(f"wD{sl}", wD)]
                            P.emit("tensor", "matmul",
                                   pw[:, g_ * 512:(g_ + 1) * 512],
                                   hab[:, dc * TB + tb * 128:dc * TB + tb * 128 + 128],
                                   w3sl[:, sl * 1024 + g_ * 512:
                                        sl * 1024 + g_ * 512 + 512],
                                   start=(dc == 0), stop=(dc == 31), wait=w,
                                   inc=("mm" if (tb4 == 3 and g_ == 1) else None))
                    L[f"w3{sl}_free"] = P.c("mm")
                y_mm = P.c("mm")
                for tb4 in range(4):
                    tb = grp * 4 + tb4
                    pw = (p01, p23, p45, p67)[tb4]
                    sl = tb % 2
                    P.emit("vector", "tensor_tensor", ysb[:], pw[:], s3r[:],
                           ALU.mult, wait=[("mm", y_mm), ("ini", ini2)],
                           inc="vx")
                    sc = stats_ln(ysb[:], EPS_LN, wait_s=[("vx", P.c("vx"))])
                    P.emit("sync", "dma_start", inc=f"xl{sl}", incn=16,
                           out=xst[:, sl * D:(sl + 1) * D],
                           in_=X2SCR[rows0 + tb * 128:rows0 + (tb + 1) * 128, :],
                           wait=[("sx", L.get(f"xst{sl}_s", 0)),
                                 ("vx", L.get(f"xst{sl}_v", 0))])
                    xlc = P.c(f"xl{sl}")
                    P.emit("vector", "tensor_scalar", ytmp[:], ysb[:],
                           stat[:, 14:15], stat[:, 15:16], ALU.subtract,
                           ALU.mult, wait=[("sx", sc)], inc="vx")
                    L[f"pw{tb4}_free"] = P.c("vx")
                    P.emit("vector", "tensor_tensor", outst[:],
                           ytmp[:], xst[:, sl * D:(sl + 1) * D], ALU.add,
                           wait=[(f"xl{sl}", xlc),
                                 ("out", L.get("outst_o", 0))], inc="vx")
                    L[f"xst{sl}_s"] = P.c("sx")
                    L[f"xst{sl}_v"] = P.c("vx")
                    P.emit("sync", "dma_start", inc="out", incn=16,
                           out=OUT[rows0 + tb * 128:rows0 + (tb + 1) * 128, :],
                           in_=outst[:], wait=[("vx", P.c("vx"))])
                    L["outst_o"] = P.c("out")
            L["pa0_free_s"] = L["pa1_free_s"] = P.c("sx")
            L["pa0_free_v"] = L["pa1_free_v"] = P.c("vx")
            L["pg0_free"] = L["pg1_free"] = P.c("sx")
            for i in range(4):
                L[f"pw{i}_free"] = P.c("vx")

        P.waitonly("sync", [("out", P.c("out"))])

        block = ctx.enter_context(nc.Block())

        @block.tensor
        def _(e):
            for fn in P.q["tensor"]:
                fn(e)

        @block.vector
        def _(e):
            for fn in P.q["vector"]:
                fn(e)

        @block.scalar
        def _(e):
            for fn in P.q["scalar"]:
                fn(e)

        @block.sync
        def _(e):
            for fn in P.q["sync"]:
                fn(e)

        @block.gpsimd
        def _(e):
            for fn in P.q["gpsimd"]:
                fn(e)

    return nc



def _rope_tables(base):
    inv = 1.0 / (base ** (np.arange(0, HD, 2, dtype=np.float64) / HD))
    pos = np.arange(S, dtype=np.float64)
    fr = pos[:, None] * inv[None, :]
    cos, sin = np.cos(fr), np.sin(fr)
    cosI = np.empty((S, HD), np.float32)
    sinI = np.empty((S, HD), np.float32)
    cosI[:, 0::2] = cos
    cosI[:, 1::2] = cos
    sinI[:, 0::2] = -sin
    sinI[:, 1::2] = sin

    def lay(a):
        return np.ascontiguousarray(
            a.reshape(NT, 128, HD).transpose(1, 0, 2).reshape(128, NT * HD))

    return lay(cosI), lay(sinI)


_CACHE = {}


def kernel(**inputs):
    bf = ml_dtypes.bfloat16
    f32 = np.float32
    g = lambda k: np.ascontiguousarray(np.asarray(inputs[k], f32))

    x = g("L0_attn_hidden")
    w1 = g("L0_ffn_w1_w")
    w2 = g("L0_ffn_w2_w")
    h1 = g("L0_ffn_h1_w")
    h2 = g("L0_ffn_h2_w")
    w3 = g("L0_ffn_w3_w")

    W1B = np.ascontiguousarray(w1.reshape(32, 128, 8, 128).transpose(0, 2, 3, 1))
    W2B = np.ascontiguousarray(w2.reshape(32, 128, 8, 128).transpose(0, 2, 3, 1))
    H1B = np.ascontiguousarray(h1.reshape(32, 128, 32, 128).transpose(0, 2, 3, 1))
    H2B = np.ascontiguousarray(h2.reshape(32, 128, 32, 128).transpose(0, 2, 3, 1))
    W3B = np.ascontiguousarray(w3.T.reshape(32, 128, D))
    lay_s = lambda k: np.ascontiguousarray(
        np.asarray(inputs[k], f32).reshape(32, 128).T)
    S3Ra = np.ascontiguousarray(
        np.broadcast_to(np.asarray(inputs["L0_ffn_w3_s"], f32)[None, :],
                        (128, D)))
    N1Ba = np.ascontiguousarray(
        np.asarray(inputs["L0_attn_norm1_w"], f32).reshape(8, 128).T)
    N2Ba = np.ascontiguousarray(
        np.asarray(inputs["L0_attn_norm2_w"], f32).reshape(8, 128).T)

    RGCa, RGSa = _rope_tables(BASE_G)
    RLCa, RLSa = _rope_tables(BASE_L)
    idx = np.arange(128)
    MGEa = (idx[None, :] >= idx[:, None]).astype(bf)
    MLTa = (idx[None, :] < idx[:, None]).astype(bf)

    if "nc" not in _CACHE:
        _CACHE["nc"] = build_program()
    nc = _CACHE["nc"]

    hp = np.asarray(HPERM)
    shared = dict(GWQ=np.ascontiguousarray(g("L0_attn_g_wq")[:, hp]),
                  GKVD=g("L0_attn_g_kv_down"),
                  GKUP=g("L0_attn_g_k_up"), GVUP=g("L0_attn_g_v_up"),
                  GWO=np.ascontiguousarray(g("L0_attn_g_wo")[hp, :]),
                  LWQ=np.ascontiguousarray(g("L0_attn_l_wq")[:, hp]),
                  LWK=g("L0_attn_l_wk"), LWV=g("L0_attn_l_wv"),
                  LWO=np.ascontiguousarray(g("L0_attn_l_wo")[hp, :]),
                  W1B=W1B, W2B=W2B, H1B=H1B, H2B=H2B, W3B=W3B,
                  S1B=lay_s("L0_ffn_w1_s"), S2B=lay_s("L0_ffn_w2_s"),
                  H1SB=lay_s("L0_ffn_h1_s"), H2SB=lay_s("L0_ffn_h2_s"),
                  S3R=S3Ra, N1B=N1Ba, N2B=N2Ba,
                  RGC=RGCa, RGS=RGSa, RLC=RLCa, RLS=RLSa, MGE=MGEa, MLT=MLTa)
    in_maps = []
    for c in range(NCORES):
        m = dict(shared)
        m["X"] = np.ascontiguousarray(
            x[c * B_CORE:(c + 1) * B_CORE].reshape(B_CORE * S, D))
        in_maps.append(m)

    res = run_bass_kernel_spmd(nc, in_maps, core_ids=list(range(NCORES)))
    out = np.concatenate(
        [np.asarray(r["OUT"]).reshape(B_CORE, S, D) for r in res.results],
        axis=0)
    return out.astype(np.float32)


# version 05
# speedup vs baseline: 1.0302x; 1.0302x over previous
from contextlib import ExitStack

import numpy as np
import ml_dtypes
import concourse.bass as bass
import concourse.mybir as mybir
from concourse.bass_utils import run_bass_kernel_spmd

F32 = mybir.dt.float32
BF16 = mybir.dt.bfloat16
AF = mybir.ActivationFunctionType
ALU = mybir.AluOpType
AX = mybir.AxisListType

NCORES = 8
B_CORE = 2
S = 2048
D = 1024
NT = S // 128
H, KVH, HD = 16, 4, 64
EPS_LN = 1e-5
EPS_RMS = 1e-6
BASE_G, BASE_L = 1e6, 1e4
TB = 1024
NFT = (B_CORE * S) // TB

HL0 = [0, 1, 2, 3, 8, 9, 10, 11]
HL1 = [4, 5, 6, 7, 12, 13, 14, 15]
QCHUNK = {}
QOFF = {}
for _c in range(8):
    QCHUNK[HL0[_c]] = _c
    QOFF[HL0[_c]] = 0
    QCHUNK[HL1[_c]] = _c
    QOFF[HL1[_c]] = 64
HPERM = []
for _c in range(8):
    HPERM += list(range(HL0[_c] * HD, HL0[_c] * HD + HD))
    HPERM += list(range(HL1[_c] * HD, HL1[_c] * HD + HD))

ENGS = ("tensor", "vector", "scalar", "sync", "gpsimd")


class Sched:
    def __init__(self, nc, ctx):
        self.nc = nc
        self.q = {e: [] for e in ENGS}
        self.sem = {}
        self.cnt = {}
        self.ctx = ctx

    def new_sem(self, name):
        self.sem[name] = self.ctx.enter_context(self.nc.semaphore(name))
        self.cnt[name] = 0

    def emit(self, eng, method, *args, wait=(), inc=None, incn=1, **kw):
        waits = [(self.sem[s], v) for (s, v) in wait if v > 0]
        ih = None
        if inc is not None:
            self.cnt[inc] += incn
            ih = (self.sem[inc], incn)

        def fn(e, method=method, args=args, kw=kw, waits=waits, ih=ih):
            for s, v in waits:
                e.wait_ge(s, v)
            ins = getattr(e, method)(*args, **kw)
            if ih is not None:
                ins.then_inc(ih[0], ih[1])

        self.q[eng].append(fn)
        return self.cnt[inc] if inc is not None else None

    def waitonly(self, eng, wait):
        waits = [(self.sem[s], v) for (s, v) in wait if v > 0]

        def fn(e, waits=waits):
            for s, v in waits:
                e.wait_ge(s, v)

        self.q[eng].append(fn)

    def c(self, name):
        return self.cnt[name]


def build_program(upto="full", dumps=()):
    nc = bass.Bass("TRN2", target_bir_lowering=False, debug=False)
    dt = nc.dram_tensor
    dump_aps = {}
    for name, shape, dty in dumps:
        dump_aps[name] = dt("DBG_" + name, shape, dty, kind="ExternalOutput").ap()

    X = dt("X", [B_CORE * S, D], F32, kind="ExternalInput").ap()
    GWQ = dt("GWQ", [D, D], F32, kind="ExternalInput").ap()
    GKVD = dt("GKVD", [D, 256], F32, kind="ExternalInput").ap()
    GKUP = dt("GKUP", [256, 256], F32, kind="ExternalInput").ap()
    GVUP = dt("GVUP", [256, 256], F32, kind="ExternalInput").ap()
    GWO = dt("GWO", [D, D], F32, kind="ExternalInput").ap()
    LWQ = dt("LWQ", [D, D], F32, kind="ExternalInput").ap()
    LWK = dt("LWK", [D, 256], F32, kind="ExternalInput").ap()
    LWV = dt("LWV", [D, 256], F32, kind="ExternalInput").ap()
    LWO = dt("LWO", [D, D], F32, kind="ExternalInput").ap()
    W1B = dt("W1B", [32, 8, 128, 128], F32, kind="ExternalInput").ap()
    W2B = dt("W2B", [32, 8, 128, 128], F32, kind="ExternalInput").ap()
    H1B = dt("H1B", [32, 32, 128, 128], F32, kind="ExternalInput").ap()
    H2B = dt("H2B", [32, 32, 128, 128], F32, kind="ExternalInput").ap()
    W3B = dt("W3B", [32, 128, D], F32, kind="ExternalInput").ap()
    S1B = dt("S1B", [128, 32], F32, kind="ExternalInput").ap()
    S2B = dt("S2B", [128, 32], F32, kind="ExternalInput").ap()
    H1SB = dt("H1SB", [128, 32], F32, kind="ExternalInput").ap()
    H2SB = dt("H2SB", [128, 32], F32, kind="ExternalInput").ap()
    S3R = dt("S3R", [128, D], F32, kind="ExternalInput").ap()
    N1B = dt("N1B", [128, 8], F32, kind="ExternalInput").ap()
    N2B = dt("N2B", [128, 8], F32, kind="ExternalInput").ap()
    RGC = dt("RGC", [128, NT * HD], F32, kind="ExternalInput").ap()
    RGS = dt("RGS", [128, NT * HD], F32, kind="ExternalInput").ap()
    RLC = dt("RLC", [128, NT * HD], F32, kind="ExternalInput").ap()
    RLS = dt("RLS", [128, NT * HD], F32, kind="ExternalInput").ap()
    MGE = dt("MGE", [128, 128], BF16, kind="ExternalInput").ap()
    MLT = dt("MLT", [128, 128], BF16, kind="ExternalInput").ap()

    OUT = dt("OUT", [B_CORE * S, D], F32, kind="ExternalOutput").ap()

    TSCR = dt("TSCR", [S, D], BF16, kind="Internal").ap()
    KSCR = dt("KSCR", [S, 256], BF16, kind="Internal").ap()
    X2SCR = dt("X2SCR", [B_CORE * S, D], F32, kind="Internal").ap()
    CW1 = dt("CW1", [32, 128, 8, 128], BF16, kind="Internal").ap()
    CW2 = dt("CW2", [32, 128, 8, 128], BF16, kind="Internal").ap()
    CH1 = dt("CH1", [32, 128, 32, 128], BF16, kind="Internal").ap()
    CH2 = dt("CH2", [32, 128, 32, 128], BF16, kind="Internal").ap()
    CW3 = dt("CW3", [32, 128, D], BF16, kind="Internal").ap()

    ctx = ExitStack()
    sb = lambda n, s, d: ctx.enter_context(nc.sbuf_tensor(n, s, d))
    psum = lambda n, s, d: ctx.enter_context(nc.psum_tensor(n, s, d))

    with ctx:
        big1 = sb("big1", [128, 16384], F32)
        big2 = sb("big2", [128, 32768], BF16)
        big3 = sb("big3", [128, 14400], BF16)
        big4 = sb("big4", [128, 6144], F32)
        xst = sb("xst", [128, 2 * D], F32)
        wfst = sb("wfst", [128, 2 * D], F32)
        stgb = sb("stgb", [128, 2 * D + 64], BF16)
        outst = sb("outst", [128, D], F32)
        mge = sb("mge", [128, 128], BF16)
        mlt = sb("mlt", [128, 128], BF16)
        n1b = sb("n1b", [128, 8], F32)
        n2b = sb("n2b", [128, 8], F32)
        ss16 = sb("ss16", [128, 32], F32)
        rq16 = sb("rq16", [128, 16], F32)
        stat = sb("stat", [128, 40], F32)

        hbuf = big1
        x1t = big2[:, 0:16384]
        wbig = big2[:, 16384:24576]
        ktb = big2[:, 24576:28672]
        latb = big2[:, 28672:32768]
        vaug = big3[:, 0:4160]
        aoutT = big3[:, 4160:8256]
        probsT = big3[:, 8256:10304]
        wkvb = big3[:, 10304:12352]
        kupb = big3[:, 12352:12864]
        vupb = big3[:, 12864:13376]
        q2b = big3[:, 13376:14400]
        rgc = big4[:, 0:1024]
        rgs = big4[:, 1024:2048]
        rlc = big4[:, 2048:3072]
        rls = big4[:, 3072:4096]
        swb = big4[:, 4096:5120]
        m1b = big4[:, 5120:6144]
        wotmp = swb
        lnz = m1b[0:1, 0:512]
        rzbs = m1b[0:64, 512:1024]
        rzb16 = swb[0:1, 0:256].bitcast(BF16)
        ones64 = stgb[0:1, 2048:2112]

        hab = big1[:].bitcast(BF16)
        g1b = big2
        hwsl = big3[:, 0:8192]
        w3sl = big3[:, 8192:12288]
        bbuf = big3[:, 12288:13312]
        w1sl = wfst[:, 0:1024].bitcast(BF16)
        w2sl = wfst[:, 1024:2048].bitcast(BF16)
        ysb = big4[:, 0:1024]
        ytmp = big4[:, 1024:2048]
        s3r = big4[:, 2048:3072]
        s1b = big4[:, 3072:3104]
        s2b = big4[:, 3104:3136]
        h1sb = big4[:, 3136:3168]
        h2sb = big4[:, 3168:3200]

        p01 = psum("p01", [128, 1024], F32)
        p23 = psum("p23", [128, 1024], F32)
        p45 = psum("p45", [128, 1024], F32)
        p67 = psum("p67", [128, 1024], F32)

        P = Sched(nc, ctx)
        for s_ in ("ini", "xl0", "xl1", "stg", "trl", "aw0", "aw1", "cv",
                   "mm", "sx", "vx", "out", "x2w", "dbg",
                   "wA0", "wA1", "wB0", "wB1", "wC0", "wC1",
                   "wD0", "wD1", "wD2", "wD3"):
            P.new_sem(s_)
        L = {}

        def dump(name, src_ap, wait=()):
            if name in dump_aps:
                P.emit("sync", "dma_start", inc="dbg", incn=16,
                       out=dump_aps[name][:], in_=src_ap, wait=list(wait))

        def finish_truncated():
            P.waitonly("sync", [("dbg", P.c("dbg")), ("mm", P.c("mm")),
                                ("vx", P.c("vx")), ("sx", P.c("sx")),
                                ("stg", P.c("stg")), ("trl", P.c("trl")),
                                ("cv", P.c("cv"))])

        for dst, src in [(rgc, RGC), (rgs, RGS), (rlc, RLC), (rls, RLS),
                         (mge, MGE), (mlt, MLT), (n1b, N1B), (n2b, N2B)]:
            P.emit("sync", "dma_start", inc="ini", incn=16, out=dst[:], in_=src[:])
        P.waitonly("sync", [("ini", P.c("ini"))])
        P.emit("vector", "memset", ones64, 1.0, inc="vx")
        P.emit("vector", "memset", stat[:, 0:1], EPS_LN, inc="vx")
        P.emit("vector", "memset", stat[:, 1:2], EPS_RMS, inc="vx")
        L["ones_v"] = P.c("vx")

        conv_plan = []
        for tag, src, dst_, n_outer in (("w1", W1B, CW1, 32),
                                        ("w2", W2B, CW2, 32),
                                        ("h1", H1B, CH1, 32),
                                        ("h2", H2B, CH2, 32),
                                        ("w3", W3B, CW3, 32)):
            for o in range(n_outer):
                conv_plan.append((tag, src, dst_, o))
        conv_idx = {}
        nconv = 0
        for tag, src, dst_, o in conv_plan:
            nconv += 1
            conv_idx[(tag, o)] = 16 * nconv
            if tag == "w3":
                ins_, outs_ = src[o], dst_[o]
            else:
                ins_, outs_ = src[o].transpose([1, 0, 2]), dst_[o]
            P.emit("gpsimd", "dma_start", inc="cv", incn=16, out=outs_, in_=ins_)
            P.waitonly("gpsimd", [("cv", 16 * nconv)])

        _VS = {"a": 4, "b": 6}
        _SB = {"a": 16, "b": 24, "r": 32}

        def stats_ln(src_ap, site, par, wait_s=(), wait_v=()):
            vs = _VS[site] + par
            base = _SB[site] + 4 * par
            w0 = [("vx", L.get(f"stv_{site}{par}", 0)),
                  ("vx", L.get("q2b_r", 0))]
            P.emit("scalar", "activation", q2b[:], src_ap, AF.Square,
                   wait=list(wait_s) + w0, accum_out=stat[:, base:base + 1],
                   inc="sx")
            P.emit("vector", "reduce_sum", stat[:, vs:vs + 1], src_ap, AX.X,
                   wait=list(wait_v) + [("sx", L.get(f"sts_{site}{par}", 0))],
                   inc="vx")
            P.emit("scalar", "mul", stat[:, base + 1:base + 2],
                   stat[:, vs:vs + 1], 1.0 / D,
                   wait=[("vx", P.c("vx"))], inc="sx")
            L[f"sts_{site}{par}"] = P.c("sx")
            P.emit("scalar", "activation", stat[:, base + 2:base + 3],
                   stat[:, base + 1:base + 2], AF.Square, inc="sx")
            P.emit("scalar", "mul", stat[:, base + 2:base + 3],
                   stat[:, base + 2:base + 3], -1.0, inc="sx")
            P.emit("scalar", "mul", stat[:, base:base + 1],
                   stat[:, base:base + 1], 1.0 / D, inc="sx")
            P.emit("scalar", "add", stat[:, base:base + 1],
                   stat[:, base:base + 1], stat[:, base + 2:base + 3], inc="sx")
            P.emit("scalar", "activation", stat[:, base + 2:base + 3],
                   stat[:, base:base + 1], AF.Ln, bias=stat[:, 0:1], inc="sx")
            sc = P.emit("scalar", "activation", stat[:, base + 3:base + 4],
                        stat[:, base + 2:base + 3], AF.Exp, scale=-0.5,
                        inc="sx")
            return sc, stat[:, base + 1:base + 2], stat[:, base + 3:base + 4]

        def stats_rms(src_ap, par, wait_s=()):
            base = _SB["r"] + 4 * par
            w0 = [("vx", L.get(f"stv_r{par}", 0)), ("vx", L.get("q2b_r", 0))]
            P.emit("scalar", "activation", q2b[:], src_ap, AF.Square,
                   wait=list(wait_s) + w0, accum_out=stat[:, base:base + 1],
                   inc="sx")
            P.emit("scalar", "activation", stat[:, base + 1:base + 2],
                   stat[:, base:base + 1], AF.Ln, scale=1.0 / D,
                   bias=stat[:, 1:2], inc="sx")
            sc = P.emit("scalar", "activation", stat[:, base + 2:base + 3],
                        stat[:, base + 1:base + 2], AF.Exp, scale=-0.5,
                        inc="sx")
            return sc, stat[:, base + 2:base + 3]

        def conv_weight(dst_ap, src_ap, scale_ap, sl):
            aw = f"aw{sl}"
            wcols = src_ap.shape[1]
            P.emit("sync", "dma_start", inc=aw, incn=16,
                   out=wfst[:, sl * D:sl * D + wcols], in_=src_ap,
                   wait=[("vx", L.get(f"awv{sl}", 0)), ("mm", L.get("wconv_mm", 0))])
            if scale_ap is not None:
                P.emit("vector", "tensor_scalar", dst_ap,
                       wfst[:, sl * D:sl * D + wcols], scale_ap, None, ALU.mult,
                       wait=[(aw, P.c(aw))], inc="vx")
            else:
                P.emit("vector", "tensor_copy", dst_ap,
                       wfst[:, sl * D:sl * D + wcols],
                       wait=[(aw, P.c(aw))], inc="vx")
            L[f"awv{sl}"] = P.c("vx")

        def rope_qknorm(psrc, width, nheads, tc, ts_, t, stg_off, scr, mm_w,
                        trl_guard):
            P.emit("scalar", "copy", swb[:, 0:width:2], psrc[:, 1:width:2],
                   wait=[("mm", mm_w), ("vx", P.c("vx"))], inc="sx")
            P.emit("scalar", "copy", swb[:, 1:width:2], psrc[:, 0:width:2],
                   inc="sx")
            swdone = P.c("sx")
            cobc = tc[:, t * HD:(t + 1) * HD].unsqueeze(1).broadcast_to(
                [128, nheads, HD])
            sibc = ts_[:, t * HD:(t + 1) * HD].unsqueeze(1).broadcast_to(
                [128, nheads, HD])
            r3 = lambda ap: ap.rearrange("p (h d) -> p h d", d=HD)
            P.emit("vector", "tensor_tensor", r3(m1b[:, 0:width]), r3(psrc),
                   cobc, ALU.mult, wait=[("mm", mm_w), ("sx", swdone)], inc="vx")
            P.emit("vector", "tensor_tensor", r3(swb[:, 0:width]),
                   r3(swb[:, 0:width]), sibc, ALU.mult, inc="vx")
            P.emit("vector", "tensor_tensor", m1b[:, 0:width], m1b[:, 0:width],
                   swb[:, 0:width], ALU.add, inc="vx")
            qr_v = P.c("vx")
            P.emit("scalar", "activation", q2b[:, 0:width], m1b[:, 0:width],
                   AF.Square, wait=[("vx", qr_v), ("vx", L.get("q2b_r", 0))],
                   inc="sx")
            P.emit("vector", "tensor_reduce", ss16[:, 0:nheads],
                   r3(q2b[:, 0:width]), AX.X, ALU.add,
                   wait=[("sx", P.c("sx"))], inc="vx")
            L["q2b_r"] = P.c("vx")
            P.emit("scalar", "activation", ss16[:, 16:16 + nheads],
                   ss16[:, 0:nheads], AF.Ln, scale=1.0 / HD, bias=stat[:, 1:2],
                   wait=[("vx", P.c("vx"))], inc="sx")
            P.emit("scalar", "activation", rq16[:, 0:nheads],
                   ss16[:, 16:16 + nheads], AF.Exp, scale=-0.5, inc="sx")
            P.emit("vector", "tensor_tensor",
                   r3(stgb[:, stg_off:stg_off + width]), r3(m1b[:, 0:width]),
                   rq16[:, 0:nheads].unsqueeze(2).broadcast_to([128, nheads, HD]),
                   ALU.mult,
                   wait=[("sx", P.c("sx")), ("stg", L.get(f"stgf{stg_off}", 0))],
                   inc="vx")
            P.emit("sync", "dma_start", inc="stg", incn=16,
                   out=scr[t * 128:(t + 1) * 128, 0:width],
                   in_=stgb[:, stg_off:stg_off + width],
                   wait=[("vx", P.c("vx")), ("trl", trl_guard)])
            L[f"stgf{stg_off}"] = P.c("stg")
            return P.c("vx"), P.c("sx")

        psq = (p01, p23)
        pkv = (p45, p67)

        def attn_weights_global():
            L["wconv_mm"] = P.c("mm")
            for c in range(8):
                conv_weight(wbig[:, c * D:(c + 1) * D],
                            GWQ[c * 128:(c + 1) * 128, :], n1b[:, c:c + 1], c % 2)
            for c in range(8):
                conv_weight(wkvb[:, c * 256:(c + 1) * 256],
                            GKVD[c * 128:(c + 1) * 128, :], n1b[:, c:c + 1], c % 2)
            for (dst, src) in ((kupb, GKUP), (vupb, GVUP)):
                for c in range(2):
                    conv_weight(dst[:, c * 256:(c + 1) * 256],
                                src[c * 128:(c + 1) * 128, :], None, c % 2)
            L["wbig_ready"] = P.c("vx")

        def attn_weights_local():
            L["wconv_mm"] = P.c("mm")
            for c in range(8):
                conv_weight(wbig[:, c * D:(c + 1) * D],
                            LWQ[c * 128:(c + 1) * 128, :], n2b[:, c:c + 1], c % 2)
            for (ofs, src) in ((0, LWK), (2048, LWV)):
                for c in range(8):
                    conv_weight(latb[:, ofs + c * 256:ofs + (c + 1) * 256],
                                src[c * 128:(c + 1) * 128, :], n2b[:, c:c + 1],
                                c % 2)
            L["wbig_ready"] = P.c("vx")
            L["lkv_ready"] = P.c("vx")

        def load_wo(src):
            L["wconv_mm"] = P.c("mm")
            for c in range(8):
                conv_weight(wbig[:, c * D:(c + 1) * D],
                            src[c * 128:(c + 1) * 128, :], None, c % 2)
            L["wo_ready"] = P.c("vx")

        def ln1_tile(b, t):
            rows0 = b * S
            sl = t % 2
            P.emit("sync", "dma_start", inc=f"xl{sl}", incn=16,
                   out=xst[:, sl * D:(sl + 1) * D],
                   in_=X[rows0 + t * 128:rows0 + (t + 1) * 128, :],
                   wait=[("sx", L.get(f"xst{sl}_s", 0)),
                         ("vx", L.get(f"xst{sl}_v", 0))])
            xlc = P.c(f"xl{sl}")
            sc, mu_ap, rs_ap = stats_ln(xst[:, sl * D:(sl + 1) * D], "a", sl,
                                        wait_s=[(f"xl{sl}", xlc)],
                                        wait_v=[(f"xl{sl}", xlc)])
            P.emit("vector", "tensor_scalar", hbuf[:, t * D:(t + 1) * D],
                   xst[:, sl * D:(sl + 1) * D], mu_ap, rs_ap,
                   ALU.subtract, ALU.mult, wait=[("sx", sc)], inc="vx")
            L["stv_a" + str(sl)] = P.c("vx")
            L[f"hbt{t}"] = P.c("vx")
            L[f"xst{sl}_s"] = P.c("sx")
            L[f"xst{sl}_v"] = P.c("vx")

        def ln1_batch(b):
            for t in range(NT):
                ln1_tile(b, t)
            L["hbuf_v"] = P.c("vx")

        def rms_stage_tile(t, trl_guard):
            htile = hbuf[:, t * D:(t + 1) * D]
            par = t % 2
            sc, rs_ap = stats_rms(htile, par,
                                  wait_s=[("vx", L.get(f"hbt{t}", 0))])
            off = (t % 2) * D
            P.emit("vector", "tensor_scalar", stgb[:, off:off + D],
                   htile, rs_ap, None, ALU.mult,
                   wait=[("sx", sc), ("stg", L.get(f"stgf{off}", 0))],
                   inc="vx")
            L["stv_r" + str(par)] = P.c("vx")
            P.emit("sync", "dma_start", inc="stg", incn=16,
                   out=TSCR[t * 128:(t + 1) * 128, :],
                   in_=stgb[:, off:off + D],
                   wait=[("vx", P.c("vx")), ("trl", trl_guard)])
            L[f"stgf{off}"] = P.c("stg")

        def x2_out_tile(b, t):
            rows0 = b * S
            sl = t % 2
            P.emit("sync", "dma_start", inc=f"xl{sl}", incn=16,
                   out=xst[:, sl * D:(sl + 1) * D],
                   in_=X[rows0 + t * 128:rows0 + (t + 1) * 128, :],
                   wait=[("sx", L.get(f"xst{sl}_s", 0)),
                         ("vx", L.get(f"xst{sl}_v", 0))])
            xlc = P.c(f"xl{sl}")
            htile = hbuf[:, t * D:(t + 1) * D]
            sc, mu_ap, rs_ap = stats_ln(htile, "b", sl,
                                        wait_s=[("vx", L.get(f"hbt{t}", 0))],
                                        wait_v=[("vx", L.get(f"hbt{t}", 0))])
            P.emit("vector", "tensor_scalar", wotmp[:], htile,
                   mu_ap, rs_ap, ALU.subtract, ALU.mult,
                   wait=[("sx", sc)], inc="vx")
            L["stv_b" + str(sl)] = P.c("vx")
            P.emit("vector", "tensor_tensor", outst[:],
                   wotmp[:], xst[:, sl * D:(sl + 1) * D], ALU.add,
                   wait=[(f"xl{sl}", xlc),
                         ("x2w", L.get("outst_f", 0))], inc="vx")
            L[f"xst{sl}_s"] = P.c("sx")
            L[f"xst{sl}_v"] = P.c("vx")
            P.emit("sync", "dma_start", inc="x2w", incn=16,
                   out=X2SCR[rows0 + t * 128:rows0 + (t + 1) * 128, :],
                   in_=outst[:], wait=[("vx", P.c("vx"))])
            L["outst_f"] = P.c("x2w")

        def attn_batch(b, glob, prestaged=False, qs_hook=None):
            tc, ts_ = (rgc, rgs) if glob else (rlc, rls)
            trl_now = P.c("trl")
            if not prestaged:
                for t in range(NT):
                    rms_stage_tile(t, trl_now)
                stg_norm = P.c("stg")
            else:
                stg_norm = L.get("prestage_stg", 0)
            if glob:
                dump(f"tscr_{b}", TSCR[:], wait=[("stg", stg_norm)])
            for c in range(8):
                P.emit("sync", "dma_start", inc="trl", incn=16,
                       out=x1t[:, c * S:(c + 1) * S],
                       in_=TSCR[:, c * 128:(c + 1) * 128], transpose=True,
                       wait=[("stg", stg_norm), ("mm", L.get("x1t_free", 0)),
                             ("sx", L.get("x1t_free_s", 0))])
            trl_x1t = P.c("trl")
            if glob:
                dump(f"x1t_{b}", x1t[:], wait=[("trl", trl_x1t)])
                dump(f"wbig_{b}", wbig[:], wait=[("vx", P.c("vx"))])
                dump(f"wkvb_{b}", wkvb[:], wait=[("vx", P.c("vx"))])
            if upto == "x1t_stop" and glob and b == 0:
                return "STOP"  

            if glob:
                for lc in range(2):
                    for tg in range(4):
                        k4 = lc * 4 + tg
                        pp = pkv[k4 % 2]
                        half = (k4 // 2) % 2
                        pslice = pp[:, half * 512:half * 512 + 512]
                        for dc in range(8):
                            w = ()
                            if dc == 0:
                                w = [("trl", trl_x1t),
                                     ("vx", L.get("wbig_ready", 0)),
                                     ("sx", L.get(f"lat{k4 % 4}_free", 0))]
                            P.emit("tensor", "matmul", pslice,
                                   wkvb[:, dc * 256 + lc * 128:
                                        dc * 256 + lc * 128 + 128],
                                   x1t[:, dc * S + tg * 512:dc * S + tg * 512 + 512],
                                   start=(dc == 0), stop=(dc == 7), wait=w,
                                   inc=("mm" if dc == 7 else None))
                        mmc = P.c("mm")
                        P.emit("scalar", "copy",
                               latb[:, lc * S + tg * 512:lc * S + tg * 512 + 512],
                               pslice, wait=[("mm", mmc)], inc="sx")
                        L[f"lat{k4 % 4}_free"] = P.c("sx")
                latdone_s = P.c("sx")
            else:
                latdone_s = P.c("sx")

            for t in range(NT):
                pq_ = psq[t % 2]
                for dc in range(8):
                    w = ()
                    if dc == 0:
                        w = [("trl", trl_x1t), ("vx", L.get("wbig_ready", 0)),
                             ("vx", L.get(f"pq{t % 2}_free_v", 0)),
                             ("sx", L.get(f"pq{t % 2}_free_s", 0))]
                    P.emit("tensor", "matmul", pq_[:, 0:512],
                           x1t[:, dc * S + t * 128:dc * S + t * 128 + 128],
                           wbig[:, dc * D:dc * D + 512],
                           start=(dc == 0), stop=(dc == 7), wait=w)
                    P.emit("tensor", "matmul", pq_[:, 512:1024],
                           x1t[:, dc * S + t * 128:dc * S + t * 128 + 128],
                           wbig[:, dc * D + 512:dc * D + 1024],
                           start=(dc == 0), stop=(dc == 7),
                           inc=("mm" if dc == 7 else None))
                q_mm = P.c("mm")
                pk_ = pkv[t % 2]
                if glob:
                    for lc in range(2):
                        w = ([("sx", latdone_s),
                              ("sx", L.get(f"pk{t % 2}_free_s", 0)),
                              ("vx", L.get(f"pk{t % 2}_free_v", 0))]
                             if lc == 0 else ())
                        P.emit("tensor", "matmul", pk_[:, 0:256],
                               latb[:, lc * S + t * 128:lc * S + t * 128 + 128],
                               kupb[:, lc * 256:(lc + 1) * 256],
                               start=(lc == 0), stop=(lc == 1), wait=w)
                    for lc in range(2):
                        P.emit("tensor", "matmul", pk_[:, 512:768],
                               latb[:, lc * S + t * 128:lc * S + t * 128 + 128],
                               vupb[:, lc * 256:(lc + 1) * 256],
                               start=(lc == 0), stop=(lc == 1),
                               inc=("mm" if lc == 1 else None))
                else:
                    for dc in range(8):
                        w = ([("vx", L.get("lkv_ready", 0)),
                              ("sx", L.get(f"pk{t % 2}_free_s", 0)),
                              ("vx", L.get(f"pk{t % 2}_free_v", 0))]
                             if dc == 0 else ())
                        P.emit("tensor", "matmul", pk_[:, 0:256],
                               x1t[:, dc * S + t * 128:dc * S + t * 128 + 128],
                               latb[:, dc * 256:(dc + 1) * 256],
                               start=(dc == 0), stop=(dc == 7), wait=w)
                    for dc in range(8):
                        P.emit("tensor", "matmul", pk_[:, 512:768],
                               x1t[:, dc * S + t * 128:dc * S + t * 128 + 128],
                               latb[:, 2048 + dc * 256:2048 + (dc + 1) * 256],
                               start=(dc == 0), stop=(dc == 7),
                               inc=("mm" if dc == 7 else None))
                kv_mm = P.c("mm")
                sl = t % 2
                v_, s_ = rope_qknorm(pq_[:, 0:D], D, H, tc, ts_, t, sl * D,
                                     TSCR, q_mm, trl_x1t)
                L[f"pq{t % 2}_free_v"] = v_
                L[f"pq{t % 2}_free_s"] = s_
                v2_, s2_ = rope_qknorm(pk_[:, 0:256], 256, KVH, tc, ts_, t,
                                       sl * D, KSCR, kv_mm, L.get("ktrl", 0))
                P.emit("scalar", "copy",
                       vaug[:, t * 260:t * 260 + 260].rearrange(
                           "p (h c) -> p h c", c=65)[:, :, 0:64],
                       pk_[:, 512:768].rearrange("p (h c) -> p h c", c=64),
                       wait=[("mm", kv_mm), ("mm", L.get("vaug_free", 0))],
                       inc="sx")
                P.emit("vector", "memset",
                       vaug[:, t * 260:t * 260 + 260].rearrange(
                           "p (h c) -> p h c", c=65)[:, :, 64:65], 1.0,
                       wait=[("sx", P.c("sx"))], inc="vx")
                L[f"pk{t % 2}_free_s"] = P.c("sx")
                L[f"pk{t % 2}_free_v"] = max(v2_, P.c("vx"))
            stg_q = P.c("stg")
            proj_mm = P.c("mm")
            L["x1t_free"] = proj_mm

            load_wo(GWO if glob else LWO)

            for c in range(8):
                P.emit("sync", "dma_start", inc="trl", incn=16,
                       out=x1t[:, c * S:(c + 1) * S],
                       in_=TSCR[:, c * 128:(c + 1) * 128], transpose=True,
                       wait=[("stg", stg_q), ("mm", proj_mm)])
            for c in range(2):
                P.emit("sync", "dma_start", inc="trl", incn=16,
                       out=ktb[:, c * S:(c + 1) * S],
                       in_=KSCR[:, c * 128:(c + 1) * 128], transpose=True,
                       wait=[("stg", stg_q), ("mm", L.get("ktb_free", 0))])
            trl_qk = P.c("trl")
            L["ktrl"] = trl_qk

            L["sc0_free_s"] = L["sc1_free_s"] = P.c("sx")
            L["sc0_free_v"] = L["sc1_free_v"] = P.c("vx")
            L["pv_free_v"] = P.c("vx")
            L["pv_free_s"] = P.c("sx")
            pvb = p01
            for qs in range(4):
                for h in range(H):
                    kvh = h // 4
                    qch, qoff = QCHUNK[h], QOFF[h]
                    kch, koff = kvh // 2, 64 * (kvh % 2)
                    if glob:
                        ktcs = list(range(0, 4 * qs + 4))
                    else:
                        ktcs = [k for k in range(4 * qs - 2, 4 * qs + 4) if k >= 0]
                    pairs = [ktcs[i:i + 2] for i in range(0, len(ktcs), 2)]

                    def rng(ktc):
                        o = 128 * ktc - 512 * qs
                        lo = max(0, o)
                        hi = 512 if glob else min(512, o + 384)
                        return o, lo, hi

                    def emit_qk(pi):
                        pj = pairs[pi]
                        pp = pkv[pi % 2]
                        for z, ktc in enumerate(pj):
                            o, lo, hi = rng(ktc)
                            w = ()
                            if z == 0:
                                w = [("trl", trl_qk),
                                     ("sx", L.get(f"sc{pi % 2}_free_s", 0)),
                                     ("vx", L.get(f"sc{pi % 2}_free_v", 0))]
                            P.emit("tensor", "matmul",
                                   pp[:, z * 512 + lo:z * 512 + hi],
                                   ktb[koff:koff + 64,
                                       kch * S + ktc * 128:kch * S + ktc * 128 + 128],
                                   x1t[qoff:qoff + 64,
                                       qch * S + qs * 512 + lo:
                                       qch * S + qs * 512 + hi],
                                   start=True, stop=True, wait=w,
                                   inc=("mm" if z == len(pj) - 1 else None))
                        return P.c("mm")

                    def emit_expv(pi, qk_mm):
                        pj = pairs[pi]
                        pp = pkv[pi % 2]
                        expz = []
                        for z, ktc in enumerate(pj):
                            o, lo, hi = rng(ktc)
                            pb = (pi % 2) * 1024 + z * 512
                            P.emit("scalar", "activation",
                                   probsT[:, pb + lo:pb + hi],
                                   pp[:, z * 512 + lo:z * 512 + hi], AF.Exp,
                                   scale=0.125,
                                   wait=([("mm", qk_mm)] if z == 0 else ()),
                                   inc="sx")
                            expz.append(P.c("sx"))
                        L[f"sc{pi % 2}_free_s"] = P.c("sx")
                        for z, ktc in enumerate(pj):
                            o, lo, hi = rng(ktc)
                            pb = (pi % 2) * 1024 + z * 512
                            wv = 0
                            if o >= 0:
                                P.emit("vector", "tensor_tensor",
                                       probsT[:, pb + o:pb + o + 128],
                                       probsT[:, pb + o:pb + o + 128],
                                       mge[:], ALU.mult,
                                       wait=[("sx", expz[z])], inc="vx")
                                wv = P.c("vx")
                            if (not glob) and 0 <= o + 256 < 512:
                                P.emit("vector", "tensor_tensor",
                                       probsT[:, pb + o + 256:pb + o + 384],
                                       probsT[:, pb + o + 256:pb + o + 384],
                                       mlt[:], ALU.mult,
                                       wait=[("sx", expz[z])], inc="vx")
                                wv = P.c("vx")
                            j = ktcs.index(ktc)
                            w = [("sx", expz[z])]
                            if wv:
                                w.append(("vx", wv))
                            if j == 0:
                                w.append(("vx", L.get("pv_free_v", 0)))
                                w.append(("sx", L.get("pv_free_s", 0)))
                            P.emit("tensor", "matmul", pvb[0:65, lo:hi],
                                   vaug[:, ktc * 260 + 65 * kvh:
                                        ktc * 260 + 65 * kvh + 65],
                                   probsT[:, pb + lo:pb + hi],
                                   start=(j == 0), stop=(j == len(ktcs) - 1),
                                   wait=w,
                                   inc=("mm" if j == len(ktcs) - 1 else None))
                        L[f"sc{pi % 2}_free_v"] = P.c("vx")

                    qkm = [emit_qk(0)]
                    for pi in range(1, len(pairs)):
                        qkm.append(emit_qk(pi))
                        emit_expv(pi - 1, qkm[pi - 1])
                    emit_expv(len(pairs) - 1, qkm[-1])
                    pv_mm = P.c("mm")

                    P.emit("scalar", "activation", lnz, pvb[64:65, 0:512],
                           AF.Ln, wait=[("mm", pv_mm)], inc="sx")
                    P.emit("scalar", "activation", rzb16, lnz, AF.Exp,
                           scale=-1.0, inc="sx")
                    rz_sx = P.c("sx")
                    P.emit("tensor", "matmul", pvb[0:64, 512:1024],
                           ones64, rzb16, start=True, stop=True,
                           wait=[("sx", rz_sx), ("sx", L.get("rzb_free", 0)),
                                 ("vx", L["ones_v"])],
                           inc="mm")
                    P.emit("scalar", "copy", rzbs, pvb[0:64, 512:1024],
                           wait=[("mm", P.c("mm")),
                                 ("vx", L.get("pv_free_v", 0))], inc="sx")
                    L["rzb_free"] = P.c("sx")
                    L["pv_free_s"] = P.c("sx")
                    P.emit("vector", "tensor_tensor",
                           aoutT[qoff:qoff + 64, 512 * qch:512 * qch + 512],
                           pvb[0:64, 0:512], rzbs, ALU.mult,
                           wait=[("sx", P.c("sx")),
                                 ("mm", L.get("aout_free", 0))], inc="vx")
                    L["pv_free_v"] = P.c("vx")
                heads_v = P.c("vx")
                for qb in range(4):
                    pw = (p23, p45)[qb % 2]
                    tglob = 4 * qs + qb
                    for r in range(8):
                        w = ()
                        if r == 0:
                            w = [("vx", heads_v), ("vx", L.get("wo_ready", 0)),
                                 ("sx", P.c("sx"))]
                        P.emit("tensor", "matmul", pw[:, 0:512],
                               aoutT[:, r * 512 + qb * 128:r * 512 + qb * 128 + 128],
                               wbig[:, r * D:r * D + 512],
                               start=(r == 0), stop=(r == 7), wait=w)
                        P.emit("tensor", "matmul", pw[:, 512:1024],
                               aoutT[:, r * 512 + qb * 128:r * 512 + qb * 128 + 128],
                               wbig[:, r * D + 512:r * D + 1024],
                               start=(r == 0), stop=(r == 7),
                               inc=("mm" if r == 7 else None))
                    wo_mm = P.c("mm")
                    P.emit("scalar", "copy", wotmp[:], pw[:],
                           wait=[("mm", wo_mm)], inc="sx")
                    P.emit("vector", "tensor_tensor",
                           hbuf[:, tglob * D:(tglob + 1) * D],
                           hbuf[:, tglob * D:(tglob + 1) * D], wotmp[:],
                           ALU.add, wait=[("sx", P.c("sx"))], inc="vx")
                    L["hbuf_wo_v"] = P.c("vx")
                    L[f"hbt{tglob}"] = P.c("vx")
                L["aout_free"] = P.c("mm")
                L["sc0_free_s"] = L["sc1_free_s"] = P.c("sx")
                L["sc0_free_v"] = L["sc1_free_v"] = P.c("vx")
                L["pv_free_v"] = P.c("vx")
                L["pv_free_s"] = P.c("sx")
                if qs_hook is not None:
                    qs_hook(qs)
            L["hbuf_v"] = P.c("vx")
            L["ktb_free"] = P.c("mm")
            L["vaug_free"] = P.c("mm")
            L["x1t_free"] = P.c("mm")
            L["x1t_free_s"] = P.c("sx")
            for i in (0, 1):
                L[f"pq{i}_free_v"] = L[f"pk{i}_free_v"] = P.c("vx")
                L[f"pq{i}_free_s"] = L[f"pk{i}_free_s"] = P.c("sx")


        def ffn_stage_tile(T, trl_guard, x2wait):
            rows0 = T * TB
            for st in range(8):
                sl = st % 2
                P.emit("sync", "dma_start", inc=f"xl{sl}", incn=16,
                       out=xst[:, sl * D:(sl + 1) * D],
                       in_=X2SCR[rows0 + st * 128:rows0 + (st + 1) * 128, :],
                       wait=[("x2w", x2wait), ("sx", L.get(f"xst{sl}_s", 0)),
                             ("vx", L.get(f"xst{sl}_v", 0))])
                xlc = P.c(f"xl{sl}")
                sc, mu_ap, rs_ap = stats_ln(xst[:, sl * D:(sl + 1) * D],
                                            "a", sl,
                                            wait_s=[(f"xl{sl}", xlc)],
                                            wait_v=[(f"xl{sl}", xlc)])
                off = sl * D
                P.emit("vector", "tensor_scalar", stgb[:, off:off + D],
                       xst[:, sl * D:(sl + 1) * D], mu_ap, rs_ap,
                       ALU.subtract, ALU.mult,
                       wait=[("sx", sc), ("stg", L.get(f"stgf{off}", 0))],
                       inc="vx")
                L["stv_a" + str(sl)] = P.c("vx")
                L[f"xst{sl}_s"] = P.c("sx")
                L[f"xst{sl}_v"] = P.c("vx")
                P.emit("sync", "dma_start", inc="stg", incn=16,
                       out=TSCR[st * 128:(st + 1) * 128, :],
                       in_=stgb[:, off:off + D],
                       wait=[("vx", P.c("vx")), ("trl", trl_guard)])
                L[f"stgf{off}"] = P.c("stg")
            L[f"ffnstg{T}"] = P.c("stg")

        def emit_all():
            attn_weights_global()
            for b in range(B_CORE):
                if b == 0:
                    ln1_batch(b)
                dump(f"h_ln1_{b}", hbuf[:], wait=[("vx", P.c("vx"))])
                if upto == f"ln1_{b}":
                    return False

                def hook_g(qs):
                    for t in range(4 * qs, 4 * qs + 4):
                        rms_stage_tile(t, P.c("trl"))
                r = attn_batch(b, glob=True, prestaged=(b > 0), qs_hook=hook_g)
                if r == "STOP":
                    return False
                L["prestage_stg"] = P.c("stg")
                dump(f"h_glob_{b}", hbuf[:], wait=[("vx", P.c("vx"))])
                dump(f"aoutT_{b}", aoutT[:], wait=[("vx", P.c("vx"))])
                dump(f"ktb_{b}", ktb[:], wait=[("trl", P.c("trl"))])
                dump(f"qt_{b}", x1t[:], wait=[("trl", P.c("trl"))])
                dump(f"latb_{b}", latb[:], wait=[("sx", P.c("sx"))])
                dump(f"vaug_{b}", vaug[:], wait=[("sx", P.c("sx")), ("vx", P.c("vx"))])
                if upto == f"glob_{b}":
                    return False
                attn_weights_local()

                def hook_l(qs, b=b):
                    for t in range(4 * qs, 4 * qs + 4):
                        x2_out_tile(b, t)
                        if b + 1 < B_CORE:
                            ln1_tile(b + 1, t)
                            rms_stage_tile(t, P.c("trl"))
                    if b + 1 == B_CORE and qs == 3:
                        ffn_stage_tile(0, P.c("trl"), P.c("x2w"))
                attn_batch(b, glob=False, prestaged=True, qs_hook=hook_l)
                L["prestage_stg"] = P.c("stg")
                dump(f"h_loc_{b}", hbuf[:], wait=[("vx", P.c("vx"))])
                if upto == f"loc_{b}":
                    return False
                if b + 1 < B_CORE:
                    attn_weights_global()
            return True
        cont = emit_all()
        if not cont:
            finish_truncated()
            blockT = ctx.enter_context(nc.Block())
            for _e in ENGS:
                def _mk(_e):
                    def _f(e):
                        for fn in P.q[_e]:
                            fn(e)
                    return _f
                getattr(blockT, _e)(_mk(_e))
            return nc
        x2done = P.c("x2w")
        dump("x2scr", X2SCR[:], wait=[("x2w", x2done)])
        if upto == "x2":
            finish_truncated()
            blockT = ctx.enter_context(nc.Block())
            for _e in ENGS:
                def _mk(_e):
                    def _f(e):
                        for fn in P.q[_e]:
                            fn(e)
                    return _f
                getattr(blockT, _e)(_mk(_e))
            return nc

        P.waitonly("sync", [("mm", P.c("mm")), ("vx", P.c("vx")),
                            ("sx", P.c("sx"))])
        for dst, src in [(s3r, S3R), (s1b, S1B), (s2b, S2B),
                         (h1sb, H1SB), (h2sb, H2SB)]:
            P.emit("sync", "dma_start", inc="ini", incn=16, out=dst[:], in_=src[:])
        ini2 = P.c("ini")
        for k in ("w1sl0_free", "w1sl1_free", "w2sl0_free", "w2sl1_free",
                  "hw0_free", "hw1_free", "w30_free", "w31_free", "x2t_free"):
            L[k] = 0

        pfA = (p01, p23)
        pfB = (p45, p67)

        def x2t_loads(T):
            for c in range(8):
                P.emit("sync", "dma_start", inc="trl", incn=16,
                       out=x1t[:, c * TB:(c + 1) * TB],
                       in_=TSCR[0:TB, c * 128:(c + 1) * 128], transpose=True,
                       wait=[("stg", L[f"ffnstg{T}"]),
                             ("mm", L.get("x2t_free", 0))])
            return P.c("trl")

        for T in range(NFT):
            rows0 = T * TB
            if T == 0:
                if "ffnstg0" not in L:
                    ffn_stage_tile(0, P.c("trl"), P.c("x2w"))
                trl_x2t = x2t_loads(0)
            else:
                trl_x2t = L["x2t_trl"]
            if T + 1 < NFT:
                ffn_stage_tile(T + 1, trl_x2t, P.c("x2w"))

            b1_v = P.c("vx")
            b1_s = P.c("sx")
            for hc in range(32):
                sl = hc % 2
                P.emit("sync", "dma_start", inc=f"wA{sl}", incn=16,
                       out=w1sl[:, sl * 1024:(sl + 1) * 1024],
                       in_=CW1[hc].rearrange("p a b -> p (a b)"),
                       wait=[("cv", conv_idx[("w1", hc)]),
                             ("mm", L.get(f"w1sl{sl}_free", 0))])
                wA = P.c(f"wA{sl}")
                P.emit("sync", "dma_start", inc=f"wB{sl}", incn=16,
                       out=w2sl[:, sl * 1024:(sl + 1) * 1024],
                       in_=CW2[hc].rearrange("p a b -> p (a b)"),
                       wait=[("cv", conv_idx[("w2", hc)]),
                             ("mm", L.get(f"w2sl{sl}_free", 0))])
                wB = P.c(f"wB{sl}")
                pa1, pa2 = (pfA if hc % 2 == 0 else pfB)
                for dc in range(8):
                    for g_ in range(2):
                        w = ()
                        if dc == 0 and g_ == 0:
                            w = [("trl", trl_x2t), (f"wA{sl}", wA),
                                 ("sx", max(L.get(f"pa{hc % 2}_free_s", 0), b1_s)),
                                 ("vx", max(L.get(f"pa{hc % 2}_free_v", 0), b1_v))]
                        P.emit("tensor", "matmul",
                               pa1[:, g_ * 512:(g_ + 1) * 512],
                               w1sl[:, sl * 1024 + dc * 128:
                                    sl * 1024 + dc * 128 + 128],
                               x1t[:, dc * TB + g_ * 512:dc * TB + g_ * 512 + 512],
                               start=(dc == 0), stop=(dc == 7), wait=w,
                               inc=("mm" if (dc == 7 and g_ == 1) else None))
                L[f"w1sl{sl}_free"] = P.c("mm")
                for dc in range(8):
                    for g_ in range(2):
                        w = [(f"wB{sl}", wB)] if (dc == 0 and g_ == 0) else ()
                        P.emit("tensor", "matmul",
                               pa2[:, g_ * 512:(g_ + 1) * 512],
                               w2sl[:, sl * 1024 + dc * 128:
                                    sl * 1024 + dc * 128 + 128],
                               x1t[:, dc * TB + g_ * 512:dc * TB + g_ * 512 + 512],
                               start=(dc == 0), stop=(dc == 7), wait=w,
                               inc=("mm" if (dc == 7 and g_ == 1) else None))
                a_mm = P.c("mm")
                L[f"w2sl{sl}_free"] = a_mm
                P.emit("scalar", "activation", q2b[:], pa1[:], AF.Silu,
                       scale=s1b[:, hc:hc + 1],
                       wait=[("mm", a_mm), ("vx", P.c("vx"))], inc="sx")
                L[f"pa{hc % 2}_free_s"] = P.c("sx")
                P.emit("vector", "tensor_scalar", bbuf[:], pa2[:],
                       s2b[:, hc:hc + 1], None, ALU.mult,
                       wait=[("mm", a_mm)], inc="vx")
                P.emit("vector", "tensor_tensor", hab[:, hc * TB:(hc + 1) * TB],
                       q2b[:], bbuf[:], ALU.mult,
                       wait=[("sx", P.c("sx"))], inc="vx")
                L["q2b_r"] = P.c("vx")
                L[f"pa{hc % 2}_free_v"] = P.c("vx")
            ha_v = P.c("vx")

            for stage in range(2):
                tag = "h1" if stage == 0 else "h2"
                srcW = CH1 if stage == 0 else CH2
                scl = h1sb if stage == 0 else h2sb
                src = hab if stage == 0 else g1b
                dst = g1b if stage == 0 else hab
                stage_s = P.c("sx")
                stage_v = P.c("vx")
                for oc in range(32):
                    sl = oc % 2
                    P.emit("sync", "dma_start", inc=f"wC{sl}", incn=16,
                           out=hwsl[:, sl * 4096:(sl + 1) * 4096],
                           in_=srcW[oc].rearrange("p a b -> p (a b)"),
                           wait=[("cv", conv_idx[(tag, oc)]),
                                 ("mm", L.get(f"hw{sl}_free", 0))])
                    wC = P.c(f"wC{sl}")
                    pg = (p01, p23)[oc % 2]
                    for dc in range(32):
                        for g_ in range(2):
                            w = ()
                            if dc == 0 and g_ == 0:
                                w = [(f"wC{sl}", wC), ("vx", stage_v),
                                     ("sx", max(L.get(f"pg{oc % 2}_free", 0),
                                                stage_s))]
                                if stage == 0:
                                    w.append(("vx", ha_v))
                                else:
                                    w.append(("mm", L.get("b2_done", 0)))
                            P.emit("tensor", "matmul",
                                   pg[:, g_ * 512:(g_ + 1) * 512],
                                   hwsl[:, sl * 4096 + dc * 128:
                                        sl * 4096 + dc * 128 + 128],
                                   src[:, dc * TB + g_ * 512:dc * TB + g_ * 512 + 512],
                                   start=(dc == 0), stop=(dc == 31), wait=w,
                                   inc=("mm" if (dc == 31 and g_ == 1) else None))
                    g_mm = P.c("mm")
                    L[f"hw{sl}_free"] = g_mm
                    P.emit("scalar", "activation", dst[:, oc * TB:(oc + 1) * TB],
                           pg[:], AF.Gelu, scale=scl[:, oc:oc + 1],
                           wait=[("mm", g_mm)], inc="sx")
                    L[f"pg{oc % 2}_free"] = P.c("sx")
                if stage == 0:
                    L["b2_done"] = P.c("mm")
            L["x2t_free"] = P.c("mm")
            if T + 1 < NFT:
                L["x2t_trl"] = x2t_loads(T + 1)
            g2_s = P.c("sx")

            b4_s = P.c("sx")
            b4_v = P.c("vx")
            for grp in range(2):
                for dc in range(32):
                    sl = dc % 4
                    P.emit("sync", "dma_start", inc=f"wD{sl}", incn=16,
                           out=w3sl[:, sl * 1024:(sl + 1) * 1024],
                           in_=CW3[dc],
                           wait=[("cv", conv_idx[("w3", dc)]),
                                 ("mm", L.get(f"w3{sl}_free", 0))])
                    wD = P.c(f"wD{sl}")
                    for tb4 in range(4):
                        tb = grp * 4 + tb4
                        pw = (p01, p23, p45, p67)[tb4]
                        for g_ in range(2):
                            w = ()
                            if dc == 0 and g_ == 0:
                                w = [(f"wD{sl}", wD),
                                     ("sx", max(g2_s, b4_s)),
                                     ("vx", max(L.get(f"pw{tb4}_free", 0), b4_v))]
                            elif tb4 == 0 and g_ == 0:
                                w = [(f"wD{sl}", wD)]
                            P.emit("tensor", "matmul",
                                   pw[:, g_ * 512:(g_ + 1) * 512],
                                   hab[:, dc * TB + tb * 128:dc * TB + tb * 128 + 128],
                                   w3sl[:, sl * 1024 + g_ * 512:
                                        sl * 1024 + g_ * 512 + 512],
                                   start=(dc == 0), stop=(dc == 31), wait=w,
                                   inc=("mm" if (tb4 == 3 and g_ == 1) else None))
                    L[f"w3{sl}_free"] = P.c("mm")
                y_mm = P.c("mm")
                for tb4 in range(4):
                    tb = grp * 4 + tb4
                    pw = (p01, p23, p45, p67)[tb4]
                    sl = tb % 2
                    P.emit("vector", "tensor_tensor", ysb[:], pw[:], s3r[:],
                           ALU.mult, wait=[("mm", y_mm), ("ini", ini2)],
                           inc="vx")
                    sc, mu_ap, rs_ap = stats_ln(ysb[:], "b", tb4 % 2,
                                                wait_s=[("vx", P.c("vx"))],
                                                wait_v=[("vx", P.c("vx"))])
                    P.emit("sync", "dma_start", inc=f"xl{sl}", incn=16,
                           out=xst[:, sl * D:(sl + 1) * D],
                           in_=X2SCR[rows0 + tb * 128:rows0 + (tb + 1) * 128, :],
                           wait=[("sx", L.get(f"xst{sl}_s", 0)),
                                 ("vx", L.get(f"xst{sl}_v", 0))])
                    xlc = P.c(f"xl{sl}")
                    P.emit("vector", "tensor_scalar", ytmp[:], ysb[:],
                           mu_ap, rs_ap, ALU.subtract,
                           ALU.mult, wait=[("sx", sc)], inc="vx")
                    L["stv_b" + str(tb4 % 2)] = P.c("vx")
                    L[f"pw{tb4}_free"] = P.c("vx")
                    P.emit("vector", "tensor_tensor", outst[:],
                           ytmp[:], xst[:, sl * D:(sl + 1) * D], ALU.add,
                           wait=[(f"xl{sl}", xlc),
                                 ("out", L.get("outst_o", 0))], inc="vx")
                    L[f"xst{sl}_s"] = P.c("sx")
                    L[f"xst{sl}_v"] = P.c("vx")
                    P.emit("sync", "dma_start", inc="out", incn=16,
                           out=OUT[rows0 + tb * 128:rows0 + (tb + 1) * 128, :],
                           in_=outst[:], wait=[("vx", P.c("vx"))])
                    L["outst_o"] = P.c("out")
            L["pa0_free_s"] = L["pa1_free_s"] = P.c("sx")
            L["pa0_free_v"] = L["pa1_free_v"] = P.c("vx")
            L["pg0_free"] = L["pg1_free"] = P.c("sx")
            for i in range(4):
                L[f"pw{i}_free"] = P.c("vx")

        P.waitonly("sync", [("out", P.c("out"))])

        block = ctx.enter_context(nc.Block())

        @block.tensor
        def _(e):
            for fn in P.q["tensor"]:
                fn(e)

        @block.vector
        def _(e):
            for fn in P.q["vector"]:
                fn(e)

        @block.scalar
        def _(e):
            for fn in P.q["scalar"]:
                fn(e)

        @block.sync
        def _(e):
            for fn in P.q["sync"]:
                fn(e)

        @block.gpsimd
        def _(e):
            for fn in P.q["gpsimd"]:
                fn(e)

    return nc



def _rope_tables(base):
    inv = 1.0 / (base ** (np.arange(0, HD, 2, dtype=np.float64) / HD))
    pos = np.arange(S, dtype=np.float64)
    fr = pos[:, None] * inv[None, :]
    cos, sin = np.cos(fr), np.sin(fr)
    cosI = np.empty((S, HD), np.float32)
    sinI = np.empty((S, HD), np.float32)
    cosI[:, 0::2] = cos
    cosI[:, 1::2] = cos
    sinI[:, 0::2] = -sin
    sinI[:, 1::2] = sin

    def lay(a):
        return np.ascontiguousarray(
            a.reshape(NT, 128, HD).transpose(1, 0, 2).reshape(128, NT * HD))

    return lay(cosI), lay(sinI)


_CACHE = {}


def kernel(**inputs):
    bf = ml_dtypes.bfloat16
    f32 = np.float32
    g = lambda k: np.ascontiguousarray(np.asarray(inputs[k], f32))

    x = g("L0_attn_hidden")
    w1 = g("L0_ffn_w1_w")
    w2 = g("L0_ffn_w2_w")
    h1 = g("L0_ffn_h1_w")
    h2 = g("L0_ffn_h2_w")
    w3 = g("L0_ffn_w3_w")

    W1B = np.ascontiguousarray(w1.reshape(32, 128, 8, 128).transpose(0, 2, 3, 1))
    W2B = np.ascontiguousarray(w2.reshape(32, 128, 8, 128).transpose(0, 2, 3, 1))
    H1B = np.ascontiguousarray(h1.reshape(32, 128, 32, 128).transpose(0, 2, 3, 1))
    H2B = np.ascontiguousarray(h2.reshape(32, 128, 32, 128).transpose(0, 2, 3, 1))
    W3B = np.ascontiguousarray(w3.T.reshape(32, 128, D))
    lay_s = lambda k: np.ascontiguousarray(
        np.asarray(inputs[k], f32).reshape(32, 128).T)
    S3Ra = np.ascontiguousarray(
        np.broadcast_to(np.asarray(inputs["L0_ffn_w3_s"], f32)[None, :],
                        (128, D)))
    N1Ba = np.ascontiguousarray(
        np.asarray(inputs["L0_attn_norm1_w"], f32).reshape(8, 128).T)
    N2Ba = np.ascontiguousarray(
        np.asarray(inputs["L0_attn_norm2_w"], f32).reshape(8, 128).T)

    RGCa, RGSa = _rope_tables(BASE_G)
    RLCa, RLSa = _rope_tables(BASE_L)
    idx = np.arange(128)
    MGEa = (idx[None, :] >= idx[:, None]).astype(bf)
    MLTa = (idx[None, :] < idx[:, None]).astype(bf)

    if "nc" not in _CACHE:
        _CACHE["nc"] = build_program()
    nc = _CACHE["nc"]

    hp = np.asarray(HPERM)
    shared = dict(GWQ=np.ascontiguousarray(g("L0_attn_g_wq")[:, hp]),
                  GKVD=g("L0_attn_g_kv_down"),
                  GKUP=g("L0_attn_g_k_up"), GVUP=g("L0_attn_g_v_up"),
                  GWO=np.ascontiguousarray(g("L0_attn_g_wo")[hp, :]),
                  LWQ=np.ascontiguousarray(g("L0_attn_l_wq")[:, hp]),
                  LWK=g("L0_attn_l_wk"), LWV=g("L0_attn_l_wv"),
                  LWO=np.ascontiguousarray(g("L0_attn_l_wo")[hp, :]),
                  W1B=W1B, W2B=W2B, H1B=H1B, H2B=H2B, W3B=W3B,
                  S1B=lay_s("L0_ffn_w1_s"), S2B=lay_s("L0_ffn_w2_s"),
                  H1SB=lay_s("L0_ffn_h1_s"), H2SB=lay_s("L0_ffn_h2_s"),
                  S3R=S3Ra, N1B=N1Ba, N2B=N2Ba,
                  RGC=RGCa, RGS=RGSa, RLC=RLCa, RLS=RLSa, MGE=MGEa, MLT=MLTa)
    in_maps = []
    for c in range(NCORES):
        m = dict(shared)
        m["X"] = np.ascontiguousarray(
            x[c * B_CORE:(c + 1) * B_CORE].reshape(B_CORE * S, D))
        in_maps.append(m)

    res = run_bass_kernel_spmd(nc, in_maps, core_ids=list(range(NCORES)))
    out = np.concatenate(
        [np.asarray(r["OUT"]).reshape(B_CORE, S, D) for r in res.results],
        axis=0)
    return out.astype(np.float32)
